# version 06
# baseline (speedup 1.0000x reference)
from collections import defaultdict
from contextlib import ExitStack

import numpy as np

import concourse.bass as bass
import concourse.mybir as mybir
from concourse.bass_utils import run_bass_kernel_spmd

F32 = mybir.dt.float32
BF16 = mybir.dt.bfloat16
F16 = mybir.dt.float16
I32 = mybir.dt.int32
U8 = mybir.dt.uint8
AF = mybir.ActivationFunctionType
ALU = mybir.AluOpType

NCORES = 8
B, S, D = 2, 16384, 2048
T_CORE = 4096
CH = 16
I_DIM = 5632
H_J = 256
I_J = 704
EPS_LN = 1e-5
EPS_RMS = 1e-6
ROPE_THETA = 10000.0
ROT = 1024

T1 = 512
T2 = 1024
N_P1 = T_CORE // T1
N_P2 = T_CORE // T2

I_SLOTS = [(Jp, kp, 64 if Jp == 5 else 128) for Jp in range(6) for kp in range(8)]
N_SLOT = len(I_SLOTS)

BPP = 212832


def perm2048():
    P = np.empty(2048, np.int64)
    for q in range(2048):
        c, p = q // 128, q % 128
        P[q] = 8 * (128 * (c // 8) + p) + (c % 8)
    return P


P2048 = perm2048()


def host_prep(L0_attn_X, L0_attn_W_in, L0_attn_b_in, L0_attn_W_out, L0_attn_b_out,
              L0_ffn_gate_up_packed, L0_ffn_gate_up_scales,
              L0_ffn_down_packed, L0_ffn_down_scales):
    P = P2048
    W_inT = np.ascontiguousarray(np.asarray(L0_attn_W_in).T)
    W_outT = np.ascontiguousarray(np.asarray(L0_attn_W_out).T)
    b_in = np.asarray(L0_attn_b_in)

    def cons_w(Wt_block):
        Wp = Wt_block[P][:, P]
        t = Wp.reshape(CH, 128, CH, 128)
        return np.ascontiguousarray(t.transpose(2, 1, 0, 3).astype(np.float32))

    w_q_cons = cons_w(W_inT[:, 0:D])
    w_kv_cons = np.concatenate(
        [cons_w(W_inT[:, D:2 * D]), cons_w(W_inT[:, 2 * D:3 * D])], axis=0)
    w_out_cons = cons_w(W_outT)

    def bias_fm(b):
        return np.ascontiguousarray(b[P].reshape(CH, 128).T.astype(np.float32))

    b_q, b_k, b_v = (bias_fm(b_in[0:D]), bias_fm(b_in[D:2 * D]),
                     bias_fm(b_in[2 * D:3 * D]))
    b_out = bias_fm(np.asarray(L0_attn_b_out))

    gu_cols = []
    for (Jp, kp, sz) in I_SLOTS:
        ii = np.array([8 * (128 * Jp + p) + kp for p in range(sz)], np.int64)
        gu_cols.append(ii)
        gu_cols.append(ii + I_DIM)
    gu_cols = np.concatenate(gu_cols)
    gup_cons = np.ascontiguousarray(np.asarray(L0_ffn_gate_up_packed).T[:, gu_cols])
    gup_sc = np.ascontiguousarray(
        np.repeat(np.asarray(L0_ffn_gate_up_scales).astype(np.float16), 16, axis=0)[:, gu_cols])
    dpk = np.ascontiguousarray(np.asarray(L0_ffn_down_packed).T[:, P])
    dsc = np.ascontiguousarray(
        np.repeat(np.asarray(L0_ffn_down_scales).astype(np.float16), 16, axis=0)[:, P])

    inv_freq = (1.0 / (np.float32(ROPE_THETA) **
                       (np.arange(0, ROT, 2, dtype=np.float32) / np.float32(ROT)))).astype(np.float32)

    def rope_tables(s0):
        pos = np.arange(s0, s0 + T_CORE, dtype=np.float32)
        idx = 4 * np.arange(128)[:, None] + np.arange(4)[None, :]
        ang = pos[None, None, :] * inv_freq[idx][:, :, None]
        return (np.ascontiguousarray(np.cos(ang).astype(np.float32).transpose(1, 0, 2)),
                np.ascontiguousarray(np.sin(ang).astype(np.float32).transpose(1, 0, 2)))

    X = np.asarray(L0_attn_X)
    shared = {
        "w_kv_cons": w_kv_cons, "w_q_cons": w_q_cons, "w_out_cons": w_out_cons,
        "b_q": b_q, "b_k": b_k, "b_v": b_v, "b_out": b_out,
        "gup_cons": gup_cons, "gup_sc": gup_sc, "dpk_cons": dpk, "dsc_cons": dsc,
    }
    in_maps = []
    for core in range(NCORES):
        bb = core // 4
        s0 = (core % 4) * T_CORE
        xt = np.ascontiguousarray(X[bb, s0:s0 + T_CORE, :].T[P])
        cos, sin = rope_tables(s0)
        m = np.zeros((128, 2), np.float32)
        m[:, bb] = 1.0
        d = dict(shared)
        d.update({"xT": xt, "cos_t": cos, "sin_t": sin, "bmask": m})
        in_maps.append(d)
    return in_maps


ENGS = ("sync", "vector", "scalar", "tensor", "gpsimd")


class Sched:
    def __init__(self, sem):
        self.ops = {e: [] for e in ENGS}
        self.cnt = defaultdict(int)
        self.sem = sem

    def on(self, eng, fn):
        self.ops[eng].append(fn)

    def inc(self, s, n=1):
        self.cnt[s] += n
        return self.cnt[s]

    def cur(self, s):
        return self.cnt[s]

    def wait(self, eng, s, val):
        if val and val > 0:
            self.on(eng, lambda e, ss=self.sem[s], v=val: e.wait_ge(ss, v))

    def do(self, eng, builder, s=None, n=1):
        if s is None:
            self.on(eng, builder)
            return None
        val = self.inc(s, n)
        self.on(eng, lambda e, b=builder, ss=self.sem[s], m=n: b(e).then_inc(ss, m))
        return val


class Region:
    def __init__(self, tensor, base, size, name):
        self.t, self.base, self.size, self.name = tensor, base, size, name
        self.off = 0

    def tile(self, cols, dtype, tag=""):
        raw = cols * mybir.dt.size(dtype)
        nb = (raw + 31) & ~31
        assert self.off + nb <= self.size, \
            f"region {self.name} overflow at {tag}: {self.off}+{nb}>{self.size}"
        v = self.t[:, self.base + self.off:self.base + self.off + raw].bitcast(dtype)
        self.off += nb
        return v

    def tiles(self, n, cols, dtype, tag=""):
        return [self.tile(cols, dtype, f"{tag}{i}") for i in range(n)]


def build_nc(n_p1=N_P1, n_p2=N_P2, debug_outs=False):
    nc = bass.Bass("TRN2", target_bir_lowering=False)

    xT = nc.dram_tensor("xT", [D, T_CORE], F32, kind="ExternalInput").ap()
    wkv_d = nc.dram_tensor("w_kv_cons", [32, 128, CH, 128], F32, kind="ExternalInput").ap()
    wq_d = nc.dram_tensor("w_q_cons", [CH, 128, CH, 128], F32, kind="ExternalInput").ap()
    wout_d = nc.dram_tensor("w_out_cons", [CH, 128, CH, 128], F32, kind="ExternalInput").ap()
    bq_d = nc.dram_tensor("b_q", [128, CH], F32, kind="ExternalInput").ap()
    bk_d = nc.dram_tensor("b_k", [128, CH], F32, kind="ExternalInput").ap()
    bv_d = nc.dram_tensor("b_v", [128, CH], F32, kind="ExternalInput").ap()
    bo_d = nc.dram_tensor("b_out", [128, CH], F32, kind="ExternalInput").ap()
    gup_d = nc.dram_tensor("gup_cons", [H_J, 2 * I_DIM], I32, kind="ExternalInput").ap()
    gsc_d = nc.dram_tensor("gup_sc", [H_J, 2 * I_DIM], F16, kind="ExternalInput").ap()
    dpk_d = nc.dram_tensor("dpk_cons", [I_J, D], I32, kind="ExternalInput").ap()
    dsc_d = nc.dram_tensor("dsc_cons", [I_J, D], F16, kind="ExternalInput").ap()
    cos_d = nc.dram_tensor("cos_t", [4, 128, T_CORE], F32, kind="ExternalInput").ap()
    sin_d = nc.dram_tensor("sin_t", [4, 128, T_CORE], F32, kind="ExternalInput").ap()
    bm_d = nc.dram_tensor("bmask", [128, 2], F32, kind="ExternalInput").ap()
    outT = nc.dram_tensor("outT", [D, T_CORE], F32, kind="ExternalOutput").ap()

    s_wkv = nc.dram_tensor("s_wkv", [32, 128, CH * 128], BF16).ap()
    s_wq = nc.dram_tensor("s_wq", [CH, 128, CH * 128], BF16).ap()
    s_wout = nc.dram_tensor("s_wout", [CH, 128, CH * 128], BF16).ap()
    GU_BLK = []
    off = 0
    for (Jp, kp, sz) in I_SLOTS:
        GU_BLK.append((off, sz)); off += CH * sz
        GU_BLK.append((off, sz)); off += CH * sz
    s_gu = nc.dram_tensor("s_gu", [128, off], BF16).ap()
    s_dn = nc.dram_tensor("s_dn", [CH, 128, N_SLOT * 128], BF16).ap()
    cc_src = nc.dram_tensor("cc_src", [2, D], F32).ap()
    cc_dst = nc.dram_tensor("cc_dst", [2, D], F32).ap()

    dbg = {}
    if debug_outs:
        for nm, shp in (("A_pre", [128, CH]), ("A_fin", [128, CH]),
                        ("x1", [D, T2]), ("ln2", [D, T2]), ("h0", [128, T2]),
                        ("kf0", [128, T1]), ("v0", [128, T1]), ("ln1", [D, T1]),
                        ("h_all", [N_SLOT * 128, T2]), ("y0", [D, T2]),
                        ("wdn0", [128, N_SLOT * 128]), ("wgu0", [128, CH * 128]),
                        ("a01w", [128, 2 * CH]), ("ccdst", [2, D]),
                        ("a01r", [128, 2 * CH])):
            dbg[nm] = nc.dram_tensor("dbg_" + nm, shp, F32, kind="ExternalOutput").ap()

    ctx = ExitStack()
    arena = ctx.enter_context(nc.sbuf_tensor("arena", [128, BPP], U8))
    sem = {s: ctx.enter_context(nc.semaphore(s)) for s in ("SQ", "GQ", "SC", "SV", "SS", "ST")}
    sch = Sched(sem)
    psum = [ctx.enter_context(nc.psum_tensor(f"ps{i}", [128, 512], F32)) for i in range(8)]

    PERS = Region(arena, 0, 3 * 1024, "PERS")
    X1R = Region(arena, 3 * 1024, 32 * 1024, "X1")
    HR = Region(arena, 35 * 1024, 96 * 1024, "H")
    LNR = Region(arena, 131 * 1024, 32 * 1024, "LN")
    FREE = Region(arena, 163 * 1024, BPP - 163 * 1024, "FREE")

    ones_red = PERS.tile(1, BF16, "ones_red")
    ones_bc = PERS.tile(128, BF16, "ones_bc")
    bqs = PERS.tile(CH, F32); bks = PERS.tile(CH, F32)
    bvs = PERS.tile(CH, F32); bos = PERS.tile(CH, F32)
    bms = PERS.tile(2, F32)
    A_acc = PERS.tile(CH, F32); A_cols = PERS.tile(CH, F32)
    A_fin = PERS.tile(CH, F32); ab_q = PERS.tile(CH, F32)
    a01 = PERS.tile(2 * CH, F32)
    junk = PERS.tile(T1, F32)

    x1_bf = X1R.tiles(CH, T2, BF16, "x1_")
    h_bf = HR.tiles(N_SLOT, T2, BF16, "h_")
    ln_bf = LNR.tiles(CH, T2, BF16, "ln_")
    ln1 = [ln_bf[c // 2][:, (c % 2) * T1:(c % 2 + 1) * T1] for c in range(CH)]
    y_t = ln_bf

    ln_s = [FREE.tile(T2, F32, "lns") for _ in range(2)]
    mb_bf = FREE.tile(T2, BF16, "mb")
    rs_bf = FREE.tile(T2, BF16, "rs")
    rn_bf = FREE.tile(T2, BF16, "rn")
    sqb = [FREE.tile(T1, BF16, "sqb") for _ in range(2)]
    vt = FREE.tile(T1, BF16, "vt")
    vt2 = FREE.tile(T1, BF16, "vt2")
    rinvb = FREE.tile(T2, BF16, "rinvb")
    slab = [FREE.tile(CH * 128, BF16, f"slab{i}") for i in range(2)]
    ostage = [FREE.tile(T1, F32, f"ostage{i}") for i in range(2)]
    dn_slab = [FREE.tile((N_SLOT // 2) * 128, BF16, f"dslab{i}") for i in range(2)]
    print(f"[build] FREE used {FREE.off}/{FREE.size}")

    p1 = Region(arena, HR.base, HR.size, "P1H")
    xbf = p1.tiles(CH, T1, BF16, "xbf")
    kf = p1.tiles(8, T1, F32, "kf")
    kbf = p1.tiles(8, T1, BF16, "kbf")
    vbf = p1.tiles(CH, T1, BF16, "vbf")
    cosb = p1.tiles(4, T1, BF16, "cos")
    sinb = p1.tiles(4, T1, BF16, "sin")
    cwin = [p1.tile(CH * 128, F32, "cwin") for _ in range(2)]
    cstage = [p1.tile(CH * 128, BF16, "cstage") for _ in range(2)]
    print(f"[build] P1/H used {p1.off}/{p1.size}")
    kr1 = Region(arena, HR.base, 16 * 1024, "KR1")
    krot_p1 = kr1.tiles(8, T1, F32, "kr1_")
    kr2 = Region(arena, HR.base + 64 * 1024, 16 * 1024, "KR2")
    krot_p2 = kr2.tiles(8, T1, F32, "kr2_")
    qf = kf
    yd = kbf + vbf[0:8]

    def mk_gu_ws(region, maxc):
        ws = {
            "stage": region.tile(CH * maxc, BF16, "gstage"),
            "pk": [region.tile(maxc, I32, f"gpk{J}") for J in range(2)],
            "sc16": region.tile(maxc, F16, "gsc16"),
            "sc": region.tile(maxc, F32, "gsc"),
            "tmp": [region.tile(maxc, I32, f"gtmp{i}") for i in range(4)],
            "sv": {}, "flush": [0], "cv": [0], "maxc": maxc,
        }
        return ws

    def mk_dn_ws(region):
        return {
            "stage": region.tile(N_SLOT * 128, BF16, "dstage"),
            "pk": [region.tile(128, I32, f"dpk{i}") for i in range(2)],
            "sc16": region.tile(128, F16, "dsc16"),
            "sc": region.tile(128, F32, "dsc"),
            "tmp": [region.tile(128, I32, f"dtmp{i}") for i in range(4)],
            "sv": {}, "flush": [0], "cv": [0],
        }

    dq1 = Region(arena, X1R.base, X1R.size, "GUDEQ_X")
    gu_ws_x = mk_gu_ws(dq1, 512)
    print(f"[build] GUDEQ/X1 used {dq1.off}/{dq1.size}")
    dqc1 = Region(arena, HR.base + 80 * 1024, 8 * 1024, "GUC_STAGE")
    dqc2 = Region(arena, HR.base + 48 * 1024, 8 * 1024, "GUC_MISC")
    gu_ws_c = {
        "stage": dqc1.tile(CH * 256, BF16, "gstageC"),
        "pk": [dqc2.tile(256, I32, f"gpkC{J}") for J in range(2)],
        "sc16": dqc2.tile(256, F16, "gsc16C"),
        "sc": dqc2.tile(256, F32, "gscC"),
        "tmp": [dqc2.tile(256, I32, f"gtmpC{i}") for i in range(4)],
        "sv": {}, "flush": [0], "cv": [0], "maxc": 256,
    }
    print(f"[build] GUC used {dqc1.off}/{dqc1.size} + {dqc2.off}/{dqc2.size}")
    dq2 = Region(arena, LNR.base + 16 * 1024, 16 * 1024, "DNDEQ_A")
    dn_ws_a = mk_dn_ws(dq2)
    print(f"[build] DNDEQ_A used {dq2.off}/{dq2.size}")
    dq3 = Region(arena, HR.base + 64 * 1024, 16 * 1024, "DNDEQ_B")
    dn_ws_b = mk_dn_ws(dq3)
    print(f"[build] DNDEQ_B used {dq3.off}/{dq3.size}")

    def sq_dma(out, in_):
        return sch.do("sync", lambda e, o=out, i=in_: e.dma_start(out=o, in_=i), "SQ", 16)

    def gq_dma(out, in_):
        return sch.do("gpsimd", lambda e, o=out, i=in_: e.dma_start(out=o, in_=i), "GQ", 16)

    def barrier():
        vals = {"SV": sch.cur("SV"), "SS": sch.cur("SS"), "ST": sch.cur("ST")}
        for e, own in (("vector", "SV"), ("scalar", "SS"), ("tensor", "ST")):
            for s, v in vals.items():
                if s != own:
                    sch.wait(e, s, v)

    def dequant_ops(pk, sc_f32, tmp, out_view, k):
        shl = 22 - 4 * k
        a = tmp[0][0:pk.shape[0], 0:pk.shape[1]]
        b = tmp[1][0:pk.shape[0], 0:pk.shape[1]]
        c = tmp[2][0:pk.shape[0], 0:pk.shape[1]]
        d = tmp[3][0:pk.shape[0], 0:pk.shape[1]]
        if shl >= 0:
            sch.do("vector", lambda e: e.tensor_scalar(
                out=a, in0=pk, scalar1=shl, scalar2=0x03C00000,
                op0=ALU.logical_shift_left, op1=ALU.bitwise_and))
        else:
            sch.do("vector", lambda e: e.tensor_scalar(
                out=a, in0=pk, scalar1=-shl, scalar2=0x03C00000,
                op0=ALU.logical_shift_right, op1=ALU.bitwise_and))
        sch.do("vector", lambda e: e.tensor_scalar(
            out=b, in0=a, scalar1=0x01C00000, scalar2=None, op0=ALU.bitwise_and))
        sch.do("vector", lambda e: e.tensor_scalar(
            out=c, in0=a, scalar1=0x02000000, scalar2=6,
            op0=ALU.bitwise_and, op1=ALU.logical_shift_left))
        sch.do("vector", lambda e: e.tensor_scalar(
            out=a, in0=b, scalar1=0x3F000000, scalar2=None, op0=ALU.add))
        sch.do("vector", lambda e: e.tensor_scalar(
            out=b, in0=c, scalar1=0x3F800000, scalar2=None, op0=ALU.add))
        gf = a.bitcast(F32)
        sch.do("vector", lambda e: e.scalar_tensor_tensor(
            out=c.bitcast(F32), in0=gf, scalar=1.0, in1=gf, op0=ALU.min, op1=ALU.add))
        sch.do("vector", lambda e: e.scalar_tensor_tensor(
            out=d.bitcast(F32), in0=c.bitcast(F32), scalar=1.0, in1=b.bitcast(F32),
            op0=ALU.subtract, op1=ALU.mult))
        dv = d.bitcast(F32)
        if len(out_view.shape) == 3:
            dv = dv.rearrange("p (b l) -> p b l", b=out_view.shape[1])
        sch.do("vector", lambda e, ov=out_view, dv=dv: e.tensor_tensor(
            out=ov, in0=dv, in1=sc_f32, op=ALU.mult), "SV")

    sch.do("vector", lambda e: e.memset(ones_red[:, :], 1.0))
    sch.do("vector", lambda e: e.memset(ones_bc[0:1, :], 1.0))
    sch.do("vector", lambda e: e.memset(A_acc[:, :], 0.0), "SV")
    for s in range(40, 48):
        sch.do("vector", lambda e, s=s: e.memset(h_bf[s][64:128, :], 0.0))
    c_bias = 0
    for dst, src in ((bqs, bq_d), (bks, bk_d), (bvs, bv_d), (bos, bo_d), (bms, bm_d)):
        c_bias = sq_dma(dst[:, :], src)
    for e in ("scalar", "vector", "tensor"):
        sch.wait(e, "SQ", c_bias)

    conv_jobs = ([("kv", oc) for oc in range(32)] + [("q", oc) for oc in range(CH)]
                 + [("out", oc) for oc in range(CH)])
    conv_ss = [0] * len(conv_jobs)
    conv_gq = [0] * len(conv_jobs)
    wkv_gq = [0] * 32
    wq_gq = [0] * CH
    wout_gq = [0] * CH

    def emit_convert(n):
        kind, oc = conv_jobs[n]
        src = {"kv": wkv_d, "q": wq_d, "out": wout_d}[kind]
        dst = {"kv": s_wkv, "q": s_wq, "out": s_wout}[kind]
        buf = n % 2
        if n >= 2:
            sch.wait("sync", "SS", conv_ss[n - 2])
        lv = sq_dma(cwin[buf][:, :], src[oc].rearrange("p c l -> p (c l)"))
        sch.wait("scalar", "SQ", lv)
        if n >= 2:
            sch.wait("scalar", "GQ", conv_gq[n - 2])
        conv_ss[n] = sch.do("scalar", lambda e, bb=buf: e.copy(
            out=cstage[bb][:, :], in_=cwin[bb][:, :]), "SS")
        sch.wait("gpsimd", "SS", conv_ss[n])
        conv_gq[n] = gq_dma(dst[oc], cstage[buf][:, :])
        {"kv": wkv_gq, "q": wq_gq, "out": wout_gq}[kind][oc] = conv_gq[n]

    for n in range(32):
        emit_convert(n)

    gu_spans = []
    i = 0
    while i < len(GU_BLK):
        w = GU_BLK[i][1]
        nspan = 2 if w == 128 else 8
        span = list(range(i, min(i + nspan, len(GU_BLK))))
        gu_spans.append((span, w))
        i += len(span)
    blk_gq = [0] * len(GU_BLK)

    def emit_gu_span(si, ws):
        span, w = gu_spans[si]
        nb = len(span)
        ncols = nb * w
        assert ncols <= ws["maxc"]
        col0 = GU_BLK[span[0]][0] // CH
        st_cols = nb * CH * w
        stv = ws["stage"][:, 0:st_cols].rearrange("p (b c l) -> p b c l", b=nb, c=CH)
        if "last" in ws["sv"]:
            sch.wait("sync", "SV", ws["sv"]["last"])
        for J in range(2):
            sq_dma(ws["pk"][J][:, 0:ncols],
                   gup_d[J * 128:(J + 1) * 128, col0:col0 + ncols])
        l0 = sq_dma(ws["sc16"][:, 0:ncols], gsc_d[0:128, col0:col0 + ncols])
        sch.wait("vector", "SQ", l0)
        sch.wait("vector", "GQ", ws["flush"][0])
        for J in range(2):
            cvv = sch.do("vector", lambda e, n_=ncols, ws=ws: e.tensor_copy(
                out=ws["sc"][:, 0:n_], in_=ws["sc16"][:, 0:n_]), "SV")
            if J == 0:
                sch.wait("sync", "SV", cvv)
                l1 = sq_dma(ws["sc16"][:, 0:ncols], gsc_d[128:256, col0:col0 + ncols])
            scv = ws["sc"][:, 0:ncols].rearrange("p (b l) -> p b l", b=nb)
            for k in range(8):
                c = J * 8 + k
                dequant_ops(ws["pk"][J][:, 0:ncols], scv, ws["tmp"], stv[:, :, c, :], k)
            if J == 0:
                sch.wait("vector", "SQ", l1)
        ws["sv"]["last"] = sch.cur("SV")
        sch.wait("gpsimd", "SV", ws["sv"]["last"])
        out0 = GU_BLK[span[0]][0]
        ws["flush"][0] = gq_dma(s_gu[:, out0:out0 + st_cols], ws["stage"][:, 0:st_cols])
        for bk in span:
            blk_gq[bk] = ws["flush"][0]

    dn_gq = [0] * CH

    def emit_dn_span(ec, ws):
        e0 = ec * 128
        stg = ws["stage"]
        sch.wait("vector", "GQ", ws["flush"][0])
        for s in range(40, 48):
            sch.do("vector", lambda e, s=s, stg=stg: e.memset(
                stg[64:128, s * 128:(s + 1) * 128], 0.0))
        for Jp in range(6):
            sz = 64 if Jp == 5 else 128
            par = Jp % 2
            sv = ws["sv"]
            if (ec, Jp - 2) in sv:
                sch.wait("sync", "SV", sv[(ec, Jp - 2)])
            elif (ec - 1, 4 + Jp) in sv:
                sch.wait("sync", "SV", sv[(ec - 1, 4 + Jp)])
            sq_dma(ws["pk"][par][0:sz, 0:128],
                   dpk_d[Jp * 128:Jp * 128 + sz, e0:e0 + 128])
            if ws["cv"][0]:
                sch.wait("sync", "SV", ws["cv"][0])
            l2 = sq_dma(ws["sc16"][0:sz, 0:128],
                        dsc_d[Jp * 128:Jp * 128 + sz, e0:e0 + 128])
            sch.wait("vector", "SQ", l2)
            ws["cv"][0] = sch.do("vector", lambda e, sz=sz, ws=ws: e.tensor_copy(
                out=ws["sc"][0:sz, 0:128], in_=ws["sc16"][0:sz, 0:128]), "SV")
            for kp in range(8):
                s = Jp * 8 + kp
                dequant_ops(ws["pk"][par][0:sz, 0:128], ws["sc"][0:sz, 0:128],
                            ws["tmp"], stg[0:sz, s * 128:(s + 1) * 128], kp)
            sv[(ec, Jp)] = sch.cur("SV")
        sch.wait("gpsimd", "SV", ws["sv"][(ec, 5)])
        ws["flush"][0] = gq_dma(s_dn[ec], stg[:, 0:N_SLOT * 128])
        dn_gq[ec] = ws["flush"][0]

    def emit_ln(src_tiles, tcols, out_tiles):
        nh = tcols // T1
        for h in range(nh):
            for c in range(CH):
                sch.do("tensor", lambda e, c=c, h=h, t=src_tiles[c]: e.matmul(
                    psum[0 + h][0:1, 0:T1], ones_red[:, 0:1], t[:, h * T1:(h + 1) * T1],
                    start=(c == 0), stop=(c == CH - 1)),
                    "ST" if c == CH - 1 else None)
        sq_mm = {}
        prev = None
        for h in range(nh):
            for c in range(CH):
                bb = sqb[c % 2]
                key = (h, c - 2) if c >= 2 else ((h - 1, CH - 2 + c) if h > 0 else None)
                if key in sq_mm:
                    sch.wait("scalar", "ST", sq_mm[key])
                vs = sch.do("scalar", lambda e, t=src_tiles[c], bb=bb, h=h: e.activation(
                    out=bb[:, 0:T1], in_=t[:, h * T1:(h + 1) * T1], func=AF.Square), "SS")
                sch.wait("tensor", "SS", vs)
                sq_mm[(h, c)] = sch.do("tensor", lambda e, c=c, h=h, bb=bb: e.matmul(
                    psum[2 + h][0:1, 0:T1], ones_red[:, 0:1], bb[:, 0:T1],
                    start=(c == 0), stop=(c == CH - 1)), "ST")
        sch.wait("vector", "ST", sch.cur("ST"))
        m_ = ln_s[0][0:1, 0:tcols]
        v_ = ln_s[1][0:1, 0:tcols]
        for h in range(nh):
            sch.do("vector", lambda e, h=h: e.tensor_scalar(
                out=m_[0:1, h * T1:(h + 1) * T1], in0=psum[0 + h][0:1, 0:T1],
                scalar1=1.0 / D, scalar2=None, op0=ALU.mult))
            sch.do("vector", lambda e, h=h: e.tensor_scalar(
                out=v_[0:1, h * T1:(h + 1) * T1], in0=psum[2 + h][0:1, 0:T1],
                scalar1=1.0 / D, scalar2=None, op0=ALU.mult))
        mv = sch.do("vector", lambda e: e.tensor_copy(out=mb_bf[0:1, 0:tcols], in_=m_), "SV")
        m2 = rs_bf[0:1, 0:tcols]
        sch.do("vector", lambda e: e.scalar_tensor_tensor(
            out=m2, in0=m_, scalar=1.0, in1=m_, op0=ALU.mult, op1=ALU.mult))
        vv = sch.do("vector", lambda e: e.scalar_tensor_tensor(
            out=v_, in0=v_, scalar=EPS_LN, in1=m2, op0=ALU.add, op1=ALU.subtract), "SV")
        sch.wait("scalar", "SV", vv)
        sch.do("scalar", lambda e: e.activation(out=v_, in_=v_, func=AF.Ln))
        s2 = sch.do("scalar", lambda e: e.activation(
            out=rs_bf[0:1, 0:tcols], in_=v_, func=AF.Exp, scale=-0.5), "SS")
        sch.wait("tensor", "SV", mv)
        sch.wait("tensor", "SS", s2)
        bcv = None
        for h in range(nh):
            sch.do("tensor", lambda e, h=h: e.matmul(
                psum[4 + h][:, 0:T1], ones_bc[0:1, :], mb_bf[0:1, h * T1:(h + 1) * T1],
                start=True, stop=True))
            bcv = sch.do("tensor", lambda e, h=h: e.matmul(
                psum[6 + h][:, 0:T1], ones_bc[0:1, :], rs_bf[0:1, h * T1:(h + 1) * T1],
                start=True, stop=True), "ST")
        sch.wait("vector", "ST", bcv)
        for h in range(nh):
            for c in range(CH):
                o = out_tiles[c][:, h * T1:(h + 1) * T1] if tcols > T1 else out_tiles[c]
                sch.do("vector", lambda e, o=o, h=h, t=src_tiles[c]: e.tensor_tensor(
                    out=o, in0=t[:, h * T1:(h + 1) * T1], in1=psum[4 + h][:, 0:T1],
                    op=ALU.subtract))
                sch.do("vector", lambda e, o=o, h=h: e.tensor_tensor(
                    out=o, in0=o, in1=psum[6 + h][:, 0:T1], op=ALU.mult),
                    "SV" if (h == nh - 1 and c == CH - 1) else None)
        return sch.cur("SV")

    slab_ring = {"n": 0, "sq": [0, 0], "st": [0, 0]}

    def load_slab(src_ap, ncols):
        n = slab_ring["n"]
        buf = n % 2
        if n >= 2:
            sch.wait("sync", "ST", slab_ring["st"][buf])
        slab_ring["sq"][buf] = sq_dma(slab[buf][:, 0:ncols], src_ap)
        slab_ring["n"] = n + 1
        return buf

    gu_order = list(range(16)) + [40, 41] + list(range(16, 40))
    N_P1_GU = len(gu_order)
    gu_emitted = 0
    ev_par = [0, 0]
    kvconv_ss = [0] * 32
    kvc_mm = [0] * 32
    for t in range(n_p1):
        t0 = t * T1
        sch.wait("gpsimd", "SV", sch.cur("SV"))
        for c in range(CH):
            gq_dma(xbf[c][:, :], xT[c * 128:(c + 1) * 128, t0:t0 + T1])
        for i in range(4):
            gq_dma(cosb[i][:, :], cos_d[i, :, t0:t0 + T1])
            gq_dma(sinb[i][:, :], sin_d[i, :, t0:t0 + T1])
        gqv = sch.cur("GQ")
        for e in ("scalar", "tensor", "vector"):
            sch.wait(e, "GQ", gqv)

        lnv = emit_ln(xbf, T1, ln1)
        sch.wait("tensor", "SV", lnv)
        if debug_outs and t == 0:
            sch.wait("gpsimd", "SV", lnv)
            for c in range(CH):
                gq_dma(dbg["ln1"][c * 128:(c + 1) * 128, :], ln1[c])

        for oc in range(32):
            sch.wait("sync", "GQ", wkv_gq[oc])
            buf = load_slab(s_wkv[oc], CH * 128)
            sch.wait("tensor", "SQ", slab_ring["sq"][buf])
            par = oc % 2
            pb = psum[4 + par]
            sch.wait("tensor", "SS", ev_par[par])
            st_val = None
            for c in range(CH):
                st_val = sch.do("tensor", lambda e, bb=buf, c=c, pb=pb: e.matmul(
                    pb[:, 0:T1], slab[bb][:, c * 128:(c + 1) * 128], ln1[c],
                    start=(c == 0), stop=(c == CH - 1)),
                    "ST" if c == CH - 1 else None)
            slab_ring["st"][buf] = st_val
            sch.wait("scalar", "ST", st_val)
            if oc < CH:
                dst = kf[oc] if oc < 8 else kbf[oc - 8]
                bias, occ = bks, oc
            else:
                dst, bias, occ = vbf[oc - CH], bvs, oc - CH
            ev_par[par] = sch.do("scalar", lambda e, pb=pb, d_=dst, bi=bias, o=occ: e.activation(
                out=d_[:, 0:T1], in_=pb[:, 0:T1], func=AF.Identity,
                bias=bi[:, o:o + 1]), "SS")
        if debug_outs and t == 0:
            sch.wait("gpsimd", "SS", sch.cur("SS"))
            gq_dma(dbg["kf0"], kf[0][:, :])
            gq_dma(dbg["v0"], vbf[0][:, :])

        sch.wait("vector", "SS", sch.cur("SS"))
        for k in range(8):
            cc = k // 2
            op2 = ALU.subtract if k % 2 == 0 else ALU.add
            sch.do("vector", lambda e, k=k, cc=cc: e.tensor_tensor(
                out=krot_p1[k][:, :], in0=kf[k ^ 1][:, :], in1=sinb[cc][:, :], op=ALU.mult))
            sch.do("vector", lambda e, k=k, cc=cc: e.tensor_tensor(
                out=junk[:, :], in0=kf[k][:, :], in1=cosb[cc][:, :], op=ALU.mult))
            sch.do("vector", lambda e, k=k, op2=op2: e.tensor_tensor(
                out=krot_p1[k][:, :], in0=junk[:, :], in1=krot_p1[k][:, :], op=op2),
                "SV" if k == 7 else None)
        rope_v = sch.cur("SV")

        sch.wait("scalar", "SV", rope_v)
        kq_mm = {}
        for c in range(CH):
            src = krot_p1[c] if c < 8 else kbf[c - 8]
            bb = sqb[c % 2]
            if c >= 2:
                sch.wait("scalar", "ST", kq_mm[c - 2])
            vs = sch.do("scalar", lambda e, s_=src, bb=bb: e.activation(
                out=bb[:, 0:T1], in_=s_[:, 0:T1], func=AF.Square), "SS")
            sch.wait("tensor", "SS", vs)
            kq_mm[c] = sch.do("tensor", lambda e, c=c, bb=bb: e.matmul(
                psum[0][0:1, 0:T1], ones_red[:, 0:1], bb[:, 0:T1],
                start=(c == 0), stop=(c == CH - 1)), "ST")
        sch.wait("vector", "ST", sch.cur("ST"))
        ssq = ln_s[0][0:1, 0:T1]
        sv1 = sch.do("vector", lambda e: e.tensor_scalar(
            out=ssq, in0=psum[0][0:1, 0:T1], scalar1=1e-24, scalar2=None, op0=ALU.max), "SV")
        sch.wait("scalar", "SV", sv1)
        sch.do("scalar", lambda e: e.activation(out=ssq, in_=ssq, func=AF.Ln))
        rv = sch.do("scalar", lambda e: e.activation(
            out=rn_bf[0:1, 0:T1], in_=ssq, func=AF.Exp, scale=-0.5), "SS")
        sch.wait("tensor", "SS", rv)
        bc = sch.do("tensor", lambda e: e.matmul(
            psum[1][:, 0:T1], ones_bc[0:1, :], rn_bf[0:1, 0:T1], start=True, stop=True), "ST")
        sch.wait("vector", "ST", bc)

        for c in range(CH):
            ksrc = krot_p1[c] if c < 8 else kbf[c - 8]
            sch.do("vector", lambda e, c=c: e.tensor_tensor(
                out=vt[:, :], in0=vbf[c][:, :], in1=psum[1][:, 0:T1], op=ALU.mult))
            sch.do("vector", lambda e, c=c, ks=ksrc: e.scalar_tensor_tensor(
                out=junk[:, :], in0=ks[:, 0:T1], scalar=1.0, in1=vt[:, :],
                op0=ALU.mult, op1=ALU.mult, accum_out=A_cols[:, c:c + 1]))
        aav = sch.do("vector", lambda e: e.tensor_tensor(
            out=A_acc[:, :], in0=A_acc[:, :], in1=A_cols[:, :], op=ALU.add), "SV")
        sch.wait("tensor", "SV", aav)

        if t >= 1:
            cstart = 32 + (t - 1) * 5
            for n in range(cstart, min(cstart + 5, len(conv_jobs))):
                emit_convert(n)
        quota = (t + 1) * N_P1_GU // n_p1
        while gu_emitted < min(quota, N_P1_GU):
            emit_gu_span(gu_order[gu_emitted], gu_ws_x)
            gu_emitted += 1

        barrier()

    for n in range(32 + (n_p1 - 1) * 5, len(conv_jobs)):
        emit_convert(n)

    sch.do("vector", lambda e: e.tensor_scalar(
        out=a01[:, 0:CH], in0=A_acc[:, :], scalar1=bms[:, 0:1], scalar2=None, op0=ALU.mult))
    av = sch.do("vector", lambda e: e.tensor_scalar(
        out=a01[:, CH:2 * CH], in0=A_acc[:, :], scalar1=bms[:, 1:2], scalar2=None,
        op0=ALU.mult), "SV")
    sch.wait("gpsimd", "SV", av)
    if debug_outs:
        gq_dma(dbg["a01w"], a01[:, :])
    ccsv = cc_src.rearrange("r (p c) -> r p c", p=128)
    gq_dma(ccsv[0], a01[:, 0:CH])
    g1 = gq_dma(ccsv[1], a01[:, CH:2 * CH])
    sch.wait("gpsimd", "GQ", g1)
    ccv = sch.do("gpsimd", lambda e: e.collective_compute(
        "AllReduce", ALU.add, replica_groups=[list(range(NCORES))],
        ins=[cc_src.opt()], outs=[cc_dst.opt()]), "SC")
    sch.wait("gpsimd", "SC", ccv)
    g2 = gq_dma(a01[:, 0:2 * CH].rearrange("p (r c) -> p r c", r=2),
                cc_dst.rearrange("r (p c) -> p r c", p=128))
    sch.wait("vector", "GQ", g2)
    jm = junk[:, 0:CH]
    jt = junk[:, CH:2 * CH]
    sch.do("vector", lambda e: e.tensor_scalar(
        out=jm, in0=a01[:, CH:2 * CH], scalar1=bms[:, 1:2], scalar2=None,
        op0=ALU.mult))
    sch.do("vector", lambda e: e.tensor_scalar(
        out=jt, in0=a01[:, 0:CH], scalar1=bms[:, 0:1], scalar2=None, op0=ALU.mult))
    sch.do("vector", lambda e: e.tensor_copy(
        out=junk[:, 128:256], in_=junk[:, 256:384]))
    sch.do("vector", lambda e: e.tensor_tensor(
        out=A_fin[:, :], in0=jt, in1=jm, op=ALU.add))
    sch.do("vector", lambda e: e.tensor_copy(
        out=junk[:, 128:256], in_=junk[:, 256:384]))
    afv = sch.do("vector", lambda e: e.tensor_tensor(
        out=ab_q[:, :], in0=A_fin[:, :], in1=bqs[:, :], op=ALU.mult), "SV")
    if debug_outs:
        sch.wait("gpsimd", "SV", afv)
        gq_dma(dbg["A_pre"], A_acc[:, :])
        gq_dma(dbg["A_fin"], A_fin[:, :])
        gq_dma(dbg["ccdst"], cc_dst)
        gq_dma(dbg["a01r"], a01[:, :])

    sch.wait("vector", "GQ", gu_ws_x["flush"][0])
    sch.wait("vector", "SS", sch.cur("SS"))
    sch.wait("vector", "SQ", sch.cur("SQ"))
    gu_ws_c["flush"][0] = sch.cur("GQ")
    dn_ws_b["flush"][0] = sch.cur("GQ")

    dn_emitted = 0
    vt_guard = [0, 0]
    dn_ring = {"n": 0, "st": [0, 0]}
    out_sq = [0, 0]
    hv_par = [0, 0]
    yev_par = [0, 0]

    for t in range(n_p2):
        for sub in range(2):
            ts0 = t * T2 + sub * T1
            sch.wait("gpsimd", "SV", sch.cur("SV"))
            for c in range(CH):
                gq_dma(xbf[c][:, :], xT[c * 128:(c + 1) * 128, ts0:ts0 + T1])
            for i in range(4):
                gq_dma(cosb[i][:, :], cos_d[i, :, ts0:ts0 + T1])
                gq_dma(sinb[i][:, :], sin_d[i, :, ts0:ts0 + T1])
            gqv = sch.cur("GQ")
            for e in ("scalar", "tensor", "vector"):
                sch.wait(e, "GQ", gqv)

            lnv = emit_ln(xbf, T1, ln1)
            sch.wait("tensor", "SV", lnv)

            for oc in range(CH):
                sch.wait("sync", "GQ", wq_gq[oc])
                buf = load_slab(s_wq[oc], CH * 128)
                sch.wait("tensor", "SQ", slab_ring["sq"][buf])
                par = oc % 2
                pb = psum[4 + par]
                sch.wait("tensor", "SS", ev_par[par])
                st_val = None
                for c in range(CH):
                    st_val = sch.do("tensor", lambda e, bb=buf, c=c, pb=pb: e.matmul(
                        pb[:, 0:T1], slab[bb][:, c * 128:(c + 1) * 128], ln1[c],
                        start=(c == 0), stop=(c == CH - 1)),
                        "ST" if c == CH - 1 else None)
                slab_ring["st"][buf] = st_val
                sch.wait("scalar", "ST", st_val)
                if oc < 8:
                    ev_par[par] = sch.do("scalar", lambda e, pb=pb, oc=oc: e.activation(
                        out=qf[oc][:, 0:T1], in_=pb[:, 0:T1], func=AF.Identity,
                        bias=bqs[:, oc:oc + 1]), "SS")
                else:
                    if t == 0 and sub == 0 and oc == 8:
                        sch.wait("scalar", "SV", afv)
                    ev_par[par] = sch.do("scalar", lambda e, pb=pb, oc=oc: e.activation(
                        out=yd[oc][:, 0:T1], in_=pb[:, 0:T1], func=AF.Identity,
                        scale=A_fin[:, oc:oc + 1], bias=ab_q[:, oc:oc + 1]), "SS")

            if t == 1 and sub == 0:
                sch.wait("vector", "GQ", dn_ws_b["flush"][0])
            sch.wait("vector", "SS", sch.cur("SS"))
            for k in range(8):
                cc = k // 2
                op2 = ALU.subtract if k % 2 == 0 else ALU.add
                sch.do("vector", lambda e, k=k, cc=cc: e.tensor_tensor(
                    out=krot_p2[k][:, :], in0=qf[k ^ 1][:, :], in1=sinb[cc][:, :], op=ALU.mult))
                sch.do("vector", lambda e, k=k, cc=cc: e.tensor_tensor(
                    out=junk[:, :], in0=qf[k][:, :], in1=cosb[cc][:, :], op=ALU.mult))
                sch.do("vector", lambda e, k=k, op2=op2: e.tensor_tensor(
                    out=krot_p2[k][:, :], in0=junk[:, :], in1=krot_p2[k][:, :], op=op2))
                sch.do("vector", lambda e, k=k: e.tensor_scalar(
                    out=yd[k][:, 0:T1], in0=krot_p2[k][:, :], scalar1=A_fin[:, k:k + 1],
                    scalar2=None, op0=ALU.mult), "SV" if k == 7 else None)
            ydv = sch.cur("SV")
            sch.wait("tensor", "SV", ydv)

            vv_par = [0, 0]
            for oc in range(CH):
                sch.wait("sync", "GQ", wout_gq[oc])
                buf = load_slab(s_wout[oc], CH * 128)
                sch.wait("tensor", "SQ", slab_ring["sq"][buf])
                par = oc % 2
                pb = psum[4 + par]
                sch.wait("tensor", "SV", vv_par[par])
                st_val = None
                for c in range(CH):
                    st_val = sch.do("tensor", lambda e, bb=buf, c=c, pb=pb: e.matmul(
                        pb[:, 0:T1], slab[bb][:, c * 128:(c + 1) * 128], yd[c][:, 0:T1],
                        start=(c == 0), stop=(c == CH - 1)),
                        "ST" if c == CH - 1 else None)
                slab_ring["st"][buf] = st_val
                sch.wait("vector", "ST", st_val)
                x1v = x1_bf[oc][:, sub * T1:(sub + 1) * T1]
                sch.do("vector", lambda e, pb=pb, oc=oc: e.tensor_tensor(
                    out=junk[:, :], in0=pb[:, 0:T1], in1=xbf[oc][:, :], op=ALU.add))
                vv_par[par] = sch.do("vector", lambda e, oc=oc, x1v=x1v: e.scalar_tensor_tensor(
                    out=x1v, in0=junk[:, :], scalar=bos[:, oc:oc + 1],
                    in1=ln1[oc], op0=ALU.add, op1=ALU.add), "SV")

            if t == 0:
                want = 8 * (sub + 1)
                while dn_emitted < CH and dn_emitted < want:
                    emit_dn_span(dn_emitted, dn_ws_a)
                    dn_emitted += 1

            barrier()

        if t == 0:
            sch.wait("vector", "GQ", dn_ws_a["flush"][0])
        lnv2 = emit_ln(x1_bf, T2, ln_bf)
        sch.wait("tensor", "SV", lnv2)
        if debug_outs and t == 0:
            sch.wait("gpsimd", "SV", lnv2)
            for c in range(CH):
                gq_dma(dbg["x1"][c * 128:(c + 1) * 128, :], x1_bf[c][:, :])
                gq_dma(dbg["ln2"][c * 128:(c + 1) * 128, :], ln_bf[c][:, :])

        for s, (Jp, kp, sz) in enumerate(I_SLOTS):
            gblk, ublk = GU_BLK[2 * s], GU_BLK[2 * s + 1]
            if t == 0:
                sch.wait("sync", "GQ", max(blk_gq[2 * s], blk_gq[2 * s + 1]))
            bufg = load_slab(s_gu[:, gblk[0]:gblk[0] + CH * sz], CH * sz)
            bufu = load_slab(s_gu[:, ublk[0]:ublk[0] + CH * sz], CH * sz)
            sch.wait("tensor", "SQ", slab_ring["sq"][bufu])
            q4 = (s % 2) * 4
            pg = [psum[q4 + 0], psum[q4 + 1]]
            pu = [psum[q4 + 2], psum[q4 + 3]]
            sch.wait("tensor", "SV", hv_par[s % 2])
            st_g = None
            for half in range(2):
                for c in range(CH):
                    st_g = sch.do("tensor", lambda e, bb=bufg, c=c, h=half, pg=pg, sz=sz: e.matmul(
                        pg[h][0:sz, :], slab[bb][:, c * sz:(c + 1) * sz],
                        ln_bf[c][:, h * T1:(h + 1) * T1],
                        start=(c == 0), stop=(c == CH - 1)),
                        "ST" if (half == 1 and c == CH - 1) else None)
            slab_ring["st"][bufg] = st_g
            st_u = None
            for half in range(2):
                for c in range(CH):
                    st_u = sch.do("tensor", lambda e, bb=bufu, c=c, h=half, pu=pu, sz=sz: e.matmul(
                        pu[h][0:sz, :], slab[bb][:, c * sz:(c + 1) * sz],
                        ln_bf[c][:, h * T1:(h + 1) * T1],
                        start=(c == 0), stop=(c == CH - 1)),
                        "ST" if (half == 1 and c == CH - 1) else None)
            slab_ring["st"][bufu] = st_u
            sch.wait("scalar", "ST", st_u)
            for half in range(2):
                vpar = (2 * s + half) % 2
                vbuf = vt if vpar == 0 else vt2
                sch.wait("scalar", "SV", vt_guard[vpar])
                sv = sch.do("scalar", lambda e, h=half, pg=pg, sz=sz, vb=vbuf: e.activation(
                    out=vb[0:sz, 0:T1], in_=pg[h][0:sz, :], func=AF.Silu), "SS")
                sch.wait("vector", "SS", sv)
                hv_par[s % 2] = sch.do("vector", lambda e, h=half, s=s, pu=pu, sz=sz, vb=vbuf: e.tensor_tensor(
                    out=h_bf[s][0:sz, h * T1:(h + 1) * T1], in0=vb[0:sz, 0:T1],
                    in1=pu[h][0:sz, :], op=ALU.mult), "SV")
                vt_guard[vpar] = hv_par[s % 2]
        if debug_outs and t == 0:
            sch.wait("gpsimd", "SV", sch.cur("SV"))
            gq_dma(dbg["h0"], h_bf[0][:, :])
            for s in range(N_SLOT):
                gq_dma(dbg["h_all"][s * 128:(s + 1) * 128, :], h_bf[s][:, :])
            gq_dma(dbg["wdn0"], s_dn[0])
            gq_dma(dbg["wgu0"], s_gu[:, 0:CH * 128])

        sch.wait("tensor", "SV", sch.cur("SV"))
        if t == 0:
            sch.wait("scalar", "GQ", dn_ws_a["flush"][0])
        HS = N_SLOT // 2
        for ec in range(CH):
            if t == 0:
                sch.wait("sync", "GQ", dn_gq[ec])
            par = ec % 2
            py = [psum[4 + 2 * par], psum[5 + 2 * par]]
            sch.wait("tensor", "SS", yev_par[par])
            st_val = None
            for hs in range(2):
                n = dn_ring["n"]
                buf = n % 2
                if n >= 2:
                    sch.wait("sync", "ST", dn_ring["st"][buf])
                dv = sq_dma(dn_slab[buf][:, :],
                            s_dn[ec][:, hs * HS * 128:(hs + 1) * HS * 128])
                dn_ring["n"] = n + 1
                sch.wait("tensor", "SQ", dv)
                for half in range(2):
                    for si in range(HS):
                        s = hs * HS + si
                        st_val = sch.do("tensor", lambda e, bb=buf, s=s, si=si, h=half, py=py: e.matmul(
                            py[h][:, :], dn_slab[bb][:, si * 128:(si + 1) * 128],
                            h_bf[s][:, h * T1:(h + 1) * T1],
                            start=(s == 0), stop=(s == N_SLOT - 1)),
                            "ST" if (h_bf and half == 1 and si == HS - 1) else None)
                dn_ring["st"][buf] = st_val
            sch.wait("scalar", "ST", st_val)
            for half in range(2):
                yev_par[par] = sch.do("scalar", lambda e, ec=ec, h=half, py=py: e.copy(
                    out=y_t[ec][:, h * T1:(h + 1) * T1], in_=py[h][:, :]), "SS")

        if debug_outs and t == 0:
            sch.wait("gpsimd", "SS", sch.cur("SS"))
            for c in range(CH):
                gq_dma(dbg["y0"][c * 128:(c + 1) * 128, :], y_t[c][:, :])
        sch.wait("scalar", "SS", sch.cur("SS"))
        ysq_mm = {}
        for h in range(2):
            for c in range(CH):
                bb = sqb[c % 2]
                key = (h, c - 2) if c >= 2 else ((h - 1, CH - 2 + c) if h > 0 else None)
                if key in ysq_mm:
                    sch.wait("scalar", "ST", ysq_mm[key])
                vs = sch.do("scalar", lambda e, c=c, h=h, bb=bb: e.activation(
                    out=bb[:, 0:T1], in_=y_t[c][:, h * T1:(h + 1) * T1], func=AF.Square), "SS")
                sch.wait("tensor", "SS", vs)
                ysq_mm[(h, c)] = sch.do("tensor", lambda e, c=c, h=h, bb=bb: e.matmul(
                    psum[0 + h][0:1, 0:T1], ones_red[:, 0:1], bb[:, 0:T1],
                    start=(c == 0), stop=(c == CH - 1)), "ST")
        sch.wait("vector", "ST", sch.cur("ST"))
        ri = ln_s[0][0:1, 0:T2]
        rv2 = None
        for h in range(2):
            rv2 = sch.do("vector", lambda e, h=h: e.tensor_scalar(
                out=ri[0:1, h * T1:(h + 1) * T1], in0=psum[0 + h][0:1, 0:T1],
                scalar1=1.0 / D, scalar2=EPS_RMS, op0=ALU.mult, op1=ALU.add),
                "SV" if h == 1 else None)
        sch.wait("scalar", "SV", rv2)
        sch.do("scalar", lambda e: e.activation(out=ri, in_=ri, func=AF.Ln))
        rr = sch.do("scalar", lambda e: e.activation(
            out=rn_bf[0:1, 0:T2], in_=ri, func=AF.Exp, scale=-0.5), "SS")
        sch.wait("tensor", "SS", rr)
        bcv = None
        for h in range(2):
            bcv = sch.do("tensor", lambda e, h=h: e.matmul(
                psum[2 + h][:, 0:T1], ones_bc[0:1, :], rn_bf[0:1, h * T1:(h + 1) * T1],
                start=True, stop=True), "ST")
        sch.wait("vector", "ST", bcv)

        for ec in range(CH):
            for h in range(2):
                buf = (2 * ec + h) % 2
                if ec >= 1:
                    sch.wait("vector", "SQ", out_sq[buf])
                sch.do("vector", lambda e, ec=ec, bb=buf, h=h: e.tensor_tensor(
                    out=ostage[bb][:, :],
                    in0=y_t[ec][:, h * T1:(h + 1) * T1],
                    in1=psum[2 + h][:, 0:T1], op=ALU.mult))
                ov = sch.do("vector", lambda e, ec=ec, bb=buf, h=h: e.tensor_tensor(
                    out=ostage[bb][:, :], in0=ostage[bb][:, :],
                    in1=x1_bf[ec][:, h * T1:(h + 1) * T1], op=ALU.add), "SV")
                sch.wait("sync", "SV", ov)
                out_sq[buf] = sq_dma(
                    outT[ec * 128:(ec + 1) * 128, t * T2 + h * T1:t * T2 + (h + 1) * T1],
                    ostage[buf][:, :])

        barrier()

    sch.wait("sync", "SQ", sch.cur("SQ"))
    sch.wait("sync", "GQ", sch.cur("GQ"))

    with nc.Block() as block:
        @block.sync
        def _(eng):
            for fn in sch.ops["sync"]:
                fn(eng)

        @block.vector
        def _(eng):
            for fn in sch.ops["vector"]:
                fn(eng)

        @block.scalar
        def _(eng):
            for fn in sch.ops["scalar"]:
                fn(eng)

        @block.tensor
        def _(eng):
            for fn in sch.ops["tensor"]:
                fn(eng)

        @block.gpsimd
        def _(eng):
            for fn in sch.ops["gpsimd"]:
                fn(eng)

    ctx.close()
    return nc


def kernel(L0_attn_X, L0_attn_W_in, L0_attn_b_in, L0_attn_W_out, L0_attn_b_out,
           L0_ffn_gate_up_packed, L0_ffn_gate_up_scales,
           L0_ffn_down_packed, L0_ffn_down_scales):
    in_maps = host_prep(L0_attn_X, L0_attn_W_in, L0_attn_b_in, L0_attn_W_out,
                        L0_attn_b_out, L0_ffn_gate_up_packed, L0_ffn_gate_up_scales,
                        L0_ffn_down_packed, L0_ffn_down_scales)
    nc = build_nc()
    res = run_bass_kernel_spmd(nc, in_maps, core_ids=list(range(NCORES)))
    out = np.empty((B, S, D), np.float32)
    P = P2048
    for core in range(NCORES):
        bb = core // 4
        s0 = (core % 4) * T_CORE
        ot = np.asarray(res.results[core]["outT"])
        out[bb, s0:s0 + T_CORE][:, P] = ot.T
    return out


if __name__ == "__main__":
    nc = build_nc()
    print("built ok, instructions:",
          sum(1 for _ in nc.all_instructions()) if hasattr(nc, "all_instructions") else "?")


# version 09 (final)
# speedup vs baseline: 1.1605x; 1.1605x over previous
from collections import defaultdict
from contextlib import ExitStack

import numpy as np

import concourse.bass as bass
import concourse.mybir as mybir
from concourse.bass_utils import run_bass_kernel_spmd

F32 = mybir.dt.float32
BF16 = mybir.dt.bfloat16
F16 = mybir.dt.float16
I32 = mybir.dt.int32
U8 = mybir.dt.uint8
AF = mybir.ActivationFunctionType
ALU = mybir.AluOpType

NCORES = 8
B, S, D = 2, 16384, 2048
T_CORE = 4096
CH = 16
I_DIM = 5632
H_J = 256
I_J = 704
EPS_LN = 1e-5
EPS_RMS = 1e-6
ROPE_THETA = 10000.0
ROT = 1024

T1 = 512
T2 = 1024
N_P1 = T_CORE // T1
N_P2 = T_CORE // T2

I_SLOTS = [(Jp, kp, 64 if Jp == 5 else 128) for Jp in range(6) for kp in range(8)]
N_SLOT = len(I_SLOTS)

BPP = 212832


def perm2048():
    P = np.empty(2048, np.int64)
    for q in range(2048):
        c, p = q // 128, q % 128
        P[q] = 8 * (128 * (c // 8) + p) + (c % 8)
    return P


P2048 = perm2048()


def host_prep(L0_attn_X, L0_attn_W_in, L0_attn_b_in, L0_attn_W_out, L0_attn_b_out,
              L0_ffn_gate_up_packed, L0_ffn_gate_up_scales,
              L0_ffn_down_packed, L0_ffn_down_scales):
    P = P2048
    W_inT = np.ascontiguousarray(np.asarray(L0_attn_W_in).T)
    W_outT = np.ascontiguousarray(np.asarray(L0_attn_W_out).T)
    b_in = np.asarray(L0_attn_b_in)

    def cons_w(Wt_block):
        Wp = Wt_block[P][:, P]
        t = Wp.reshape(CH, 128, CH, 128)
        return np.ascontiguousarray(t.transpose(2, 1, 0, 3).astype(np.float32))

    w_q_cons = cons_w(W_inT[:, 0:D])
    w_kv_cons = np.concatenate(
        [cons_w(W_inT[:, D:2 * D]), cons_w(W_inT[:, 2 * D:3 * D])], axis=0)
    w_out_cons = cons_w(W_outT)

    def bias_fm(b):
        return np.ascontiguousarray(b[P].reshape(CH, 128).T.astype(np.float32))

    b_q, b_k, b_v = (bias_fm(b_in[0:D]), bias_fm(b_in[D:2 * D]),
                     bias_fm(b_in[2 * D:3 * D]))
    b_out = bias_fm(np.asarray(L0_attn_b_out))

    gu_cols = []
    for (Jp, kp, sz) in I_SLOTS:
        ii = np.array([8 * (128 * Jp + p) + kp for p in range(sz)], np.int64)
        gu_cols.append(ii)
        gu_cols.append(ii + I_DIM)
    gu_cols = np.concatenate(gu_cols)
    gup_cons = np.asarray(L0_ffn_gate_up_packed).T[:, gu_cols]
    gup_sc = np.repeat(np.asarray(L0_ffn_gate_up_scales).astype(np.float16),
                       16, axis=0)[:, gu_cols]
    dpk = np.asarray(L0_ffn_down_packed).T[:, P]
    dsc = np.repeat(np.asarray(L0_ffn_down_scales).astype(np.float16), 16, axis=0)[:, P]

    def gu_shard(arr, c):
        return np.ascontiguousarray(np.concatenate(
            [arr[:, 1280 * c:1280 * (c + 1)],
             arr[:, 10240 + 128 * c:10240 + 128 * (c + 1)]], axis=1))

    inv_freq = (1.0 / (np.float32(ROPE_THETA) **
                       (np.arange(0, ROT, 2, dtype=np.float32) / np.float32(ROT)))).astype(np.float32)

    def rope_tables(s0):
        pos = np.arange(s0, s0 + T_CORE, dtype=np.float32)
        idx = 4 * np.arange(128)[:, None] + np.arange(4)[None, :]
        ang = pos[None, None, :] * inv_freq[idx][:, :, None]
        return (np.ascontiguousarray(np.cos(ang).astype(np.float32).transpose(1, 0, 2)),
                np.ascontiguousarray(np.sin(ang).astype(np.float32).transpose(1, 0, 2)))

    X = np.asarray(L0_attn_X)
    shared = {
        "w_kv_cons": w_kv_cons, "w_q_cons": w_q_cons, "w_out_cons": w_out_cons,
        "b_q": b_q, "b_k": b_k, "b_v": b_v, "b_out": b_out,
    }
    in_maps = []
    for core in range(NCORES):
        bb = core // 4
        s0 = (core % 4) * T_CORE
        xt = np.ascontiguousarray(X[bb, s0:s0 + T_CORE, :].T[P])
        cos, sin = rope_tables(s0)
        m = np.zeros((128, 2), np.float32)
        m[:, bb] = 1.0
        d = dict(shared)
        d.update({"xT": xt, "cos_t": cos, "sin_t": sin, "bmask": m,
                  "gup_my": gu_shard(gup_cons, core),
                  "gsc_my": gu_shard(gup_sc, core),
                  "dpk_my": np.ascontiguousarray(dpk[:, 256 * core:256 * (core + 1)]),
                  "dsc_my": np.ascontiguousarray(dsc[:, 256 * core:256 * (core + 1)])})
        in_maps.append(d)
    return in_maps


ENGS = ("sync", "vector", "scalar", "tensor", "gpsimd")


class Sched:
    def __init__(self, sem):
        self.ops = {e: [] for e in ENGS}
        self.cnt = defaultdict(int)
        self.sem = sem

    def on(self, eng, fn):
        self.ops[eng].append(fn)

    def inc(self, s, n=1):
        self.cnt[s] += n
        return self.cnt[s]

    def cur(self, s):
        return self.cnt[s]

    def wait(self, eng, s, val):
        if val and val > 0:
            self.on(eng, lambda e, ss=self.sem[s], v=val: e.wait_ge(ss, v))

    def do(self, eng, builder, s=None, n=1):
        if s is None:
            self.on(eng, builder)
            return None
        val = self.inc(s, n)
        self.on(eng, lambda e, b=builder, ss=self.sem[s], m=n: b(e).then_inc(ss, m))
        return val


class Region:
    def __init__(self, tensor, base, size, name):
        self.t, self.base, self.size, self.name = tensor, base, size, name
        self.off = 0

    def tile(self, cols, dtype, tag=""):
        raw = cols * mybir.dt.size(dtype)
        nb = (raw + 31) & ~31
        assert self.off + nb <= self.size, \
            f"region {self.name} overflow at {tag}: {self.off}+{nb}>{self.size}"
        v = self.t[:, self.base + self.off:self.base + self.off + raw].bitcast(dtype)
        self.off += nb
        return v

    def tiles(self, n, cols, dtype, tag=""):
        return [self.tile(cols, dtype, f"{tag}{i}") for i in range(n)]


def build_nc(n_p1=N_P1, n_p2=N_P2, debug_outs=False):
    nc = bass.Bass("TRN2", target_bir_lowering=False)

    xT = nc.dram_tensor("xT", [D, T_CORE], F32, kind="ExternalInput").ap()
    wkv_d = nc.dram_tensor("w_kv_cons", [32, 128, CH, 128], F32, kind="ExternalInput").ap()
    wq_d = nc.dram_tensor("w_q_cons", [CH, 128, CH, 128], F32, kind="ExternalInput").ap()
    wout_d = nc.dram_tensor("w_out_cons", [CH, 128, CH, 128], F32, kind="ExternalInput").ap()
    bq_d = nc.dram_tensor("b_q", [128, CH], F32, kind="ExternalInput").ap()
    bk_d = nc.dram_tensor("b_k", [128, CH], F32, kind="ExternalInput").ap()
    bv_d = nc.dram_tensor("b_v", [128, CH], F32, kind="ExternalInput").ap()
    bo_d = nc.dram_tensor("b_out", [128, CH], F32, kind="ExternalInput").ap()
    gup_d = nc.dram_tensor("gup_my", [H_J, 1408], I32, kind="ExternalInput").ap()
    gsc_d = nc.dram_tensor("gsc_my", [H_J, 1408], F16, kind="ExternalInput").ap()
    dpk_d = nc.dram_tensor("dpk_my", [I_J, 256], I32, kind="ExternalInput").ap()
    dsc_d = nc.dram_tensor("dsc_my", [I_J, 256], F16, kind="ExternalInput").ap()
    cos_d = nc.dram_tensor("cos_t", [4, 128, T_CORE], F32, kind="ExternalInput").ap()
    sin_d = nc.dram_tensor("sin_t", [4, 128, T_CORE], F32, kind="ExternalInput").ap()
    bm_d = nc.dram_tensor("bmask", [128, 2], F32, kind="ExternalInput").ap()
    outT = nc.dram_tensor("outT", [D, T_CORE], F32, kind="ExternalOutput").ap()

    s_wkv = nc.dram_tensor("s_wkv", [32, 128, CH * 128], BF16).ap()
    s_wq = nc.dram_tensor("s_wq", [CH, 128, CH * 128], BF16).ap()
    s_wout = nc.dram_tensor("s_wout", [CH, 128, CH * 128], BF16).ap()
    L_BLK = [(i * 2048, 128, i * 128) for i in range(10)] + \
            [(20480, 64, 1280), (21504, 64, 1344)]
    SHARD_COLS = 22528
    s_gu_my = nc.dram_tensor("s_gu_my", [128, SHARD_COLS], BF16).ap()
    s_gu_full = nc.dram_tensor("s_gu_full", [8 * 128, SHARD_COLS], BF16).ap()
    s_dn_my = nc.dram_tensor("s_dn_my", [2 * 128, N_SLOT * 128], BF16).ap()
    s_dn_full = nc.dram_tensor("s_dn_full", [CH * 128, N_SLOT * 128], BF16).ap()

    def gu_block_addr(b):
        if b < 80:
            sh, loc = b // 10, (b % 10) * 2048
        else:
            hb = b - 80
            sh, loc = hb // 2, 20480 + (hb % 2) * 1024
        return sh * 128, loc
    cc_src = nc.dram_tensor("cc_src", [2, D], F32).ap()
    cc_dst = nc.dram_tensor("cc_dst", [2, D], F32).ap()

    dbg = {}
    if debug_outs:
        for nm, shp in (("A_pre", [128, CH]), ("A_fin", [128, CH]),
                        ("x1", [D, T2]), ("ln2", [D, T2]), ("h0", [128, T2]),
                        ("kf0", [128, T1]), ("v0", [128, T1]), ("ln1", [D, T1]),
                        ("h_all", [N_SLOT * 128, T2]), ("y0", [D, T2]),
                        ("wdn0", [128, N_SLOT * 128]), ("wgu0", [128, CH * 128]),
                        ("a01w", [128, 2 * CH]), ("ccdst", [2, D]),
                        ("a01r", [128, 2 * CH]), ("wdnall", [CH * 128, N_SLOT * 128]),
                        ("wdnmy", [2 * 128, N_SLOT * 128]),
                        ("wguall", [8 * 128, 22528]),
                        ("dnstg0", [128, N_SLOT * 128]),
                        ("dnstg1", [128, N_SLOT * 128])):
            dbg[nm] = nc.dram_tensor("dbg_" + nm, shp, F32, kind="ExternalOutput").ap()

    ctx = ExitStack()
    arena = ctx.enter_context(nc.sbuf_tensor("arena", [128, BPP], U8))
    sem = {s: ctx.enter_context(nc.semaphore(s))
           for s in ("SQ", "GQ", "SC", "SV", "SS", "ST", "SD")}
    sch = Sched(sem)
    psum = [ctx.enter_context(nc.psum_tensor(f"ps{i}", [128, 512], F32)) for i in range(8)]

    PERS = Region(arena, 0, 3 * 1024, "PERS")
    X1R = Region(arena, 3 * 1024, 32 * 1024, "X1")
    HR = Region(arena, 35 * 1024, 96 * 1024, "H")
    LNR = Region(arena, 131 * 1024, 32 * 1024, "LN")
    FREE = Region(arena, 163 * 1024, BPP - 163 * 1024, "FREE")

    ones_red = PERS.tile(1, BF16, "ones_red")
    ones_bc = PERS.tile(128, BF16, "ones_bc")
    bqs = PERS.tile(CH, F32); bks = PERS.tile(CH, F32)
    bvs = PERS.tile(CH, F32); bos = PERS.tile(CH, F32)
    bms = PERS.tile(2, F32)
    A_acc = PERS.tile(CH, F32); A_cols = PERS.tile(CH, F32)
    A_fin = PERS.tile(CH, F32); ab_q = PERS.tile(CH, F32)
    a01 = PERS.tile(2 * CH, F32)
    junk = PERS.tile(T1, F32)

    x1_bf = X1R.tiles(CH, T2, BF16, "x1_")
    h_bf = HR.tiles(N_SLOT, T2, BF16, "h_")
    ln_bf = LNR.tiles(CH, T2, BF16, "ln_")
    ln1 = [ln_bf[c // 2][:, (c % 2) * T1:(c % 2 + 1) * T1] for c in range(CH)]
    y_t = ln_bf

    ln_s = [FREE.tile(T2, F32, "lns") for _ in range(2)]
    mb_bf = FREE.tile(T2, BF16, "mb")
    rs_bf = FREE.tile(T2, BF16, "rs")
    rn_bf = FREE.tile(T2, BF16, "rn")
    sqb = [FREE.tile(T1, BF16, "sqb") for _ in range(2)]
    vt = FREE.tile(T1, BF16, "vt")
    vt2 = FREE.tile(T1, BF16, "vt2")
    rinvb = FREE.tile(T2, BF16, "rinvb")
    slab = [FREE.tile(CH * 128, BF16, f"slab{i}") for i in range(2)]
    ostage = [FREE.tile(T1, F32, f"ostage{i}") for i in range(2)]
    dn_slab = [FREE.tile((N_SLOT // 2) * 128, BF16, f"dslab{i}") for i in range(2)]
    print(f"[build] FREE used {FREE.off}/{FREE.size}")

    p1 = Region(arena, HR.base, HR.size, "P1H")
    xbf = p1.tiles(CH, T1, BF16, "xbf")
    kf = p1.tiles(8, T1, F32, "kf")
    kbf = p1.tiles(8, T1, BF16, "kbf")
    vbf = p1.tiles(CH, T1, BF16, "vbf")
    cosb = p1.tiles(4, T1, BF16, "cos")
    sinb = p1.tiles(4, T1, BF16, "sin")
    cwin = [p1.tile(CH * 128, F32, "cwin") for _ in range(2)]
    cstage = [p1.tile(CH * 128, BF16, "cstage") for _ in range(2)]
    print(f"[build] P1/H used {p1.off}/{p1.size}")
    kr1 = Region(arena, HR.base, 16 * 1024, "KR1")
    krot_p1 = kr1.tiles(8, T1, F32, "kr1_")
    kr2 = Region(arena, HR.base + 64 * 1024, 16 * 1024, "KR2")
    krot_p2 = kr2.tiles(8, T1, F32, "kr2_")
    qf = kf
    yd = kbf + vbf[0:8]

    def mk_gu_ws(region, maxc):
        ws = {
            "stage": region.tile(CH * maxc, BF16, "gstage"),
            "pk": [region.tile(maxc, I32, f"gpk{J}") for J in range(2)],
            "sc16": region.tile(maxc, F16, "gsc16"),
            "sc": region.tile(maxc, F32, "gsc"),
            "tmp": [region.tile(maxc, I32, f"gtmp{i}") for i in range(4)],
            "sv": {}, "flush": [0], "cv": [0], "maxc": maxc,
        }
        return ws

    def mk_dn_ws(region):
        return {
            "stage": region.tile(N_SLOT * 128, BF16, "dstage"),
            "pk": [region.tile(128, I32, f"dpk{i}") for i in range(2)],
            "sc16": region.tile(128, F16, "dsc16"),
            "sc": region.tile(128, F32, "dsc"),
            "tmp": [region.tile(128, I32, f"dtmp{i}") for i in range(4)],
            "sv": {}, "flush": [0], "cv": [0],
        }

    dq1 = Region(arena, X1R.base, X1R.size, "GUDEQ_X")
    gu_ws_x = mk_gu_ws(dq1, 512)
    print(f"[build] GUDEQ/X1 used {dq1.off}/{dq1.size}")
    gu_xtmp = [ostage[0].bitcast(I32)[:, 0:256], ostage[0].bitcast(I32)[:, 256:512],
               ostage[1].bitcast(I32)[:, 0:256], ostage[1].bitcast(I32)[:, 256:512]]
    dn_xtmp = [rinvb.bitcast(I32)[:, 0:128], rinvb.bitcast(I32)[:, 128:256],
               dn_slab[0].bitcast(I32)[:, 0:128], dn_slab[0].bitcast(I32)[:, 128:256],
               dn_slab[0].bitcast(I32)[:, 256:384], dn_slab[0].bitcast(I32)[:, 384:512],
               dn_slab[0].bitcast(I32)[:, 512:640], dn_slab[0].bitcast(I32)[:, 640:768],
               dn_slab[1].bitcast(I32)[:, 0:128], dn_slab[1].bitcast(I32)[:, 128:256],
               dn_slab[1].bitcast(I32)[:, 256:384], dn_slab[1].bitcast(I32)[:, 384:512]]
    dq2 = Region(arena, LNR.base + 16 * 1024, 16 * 1024, "DNDEQ_A")
    dn_ws_a = mk_dn_ws(dq2)
    print(f"[build] DNDEQ_A used {dq2.off}/{dq2.size}")

    def sq_dma(out, in_):
        return sch.do("sync", lambda e, o=out, i=in_: e.dma_start(out=o, in_=i), "SQ", 16)

    def gq_dma(out, in_):
        return sch.do("gpsimd", lambda e, o=out, i=in_: e.dma_start(out=o, in_=i), "GQ", 16)

    def sd_dma(out, in_):
        return sch.do("sync", lambda e, o=out, i=in_: e.dma_start(out=o, in_=i), "SD", 16)

    def barrier():
        vals = {"SV": sch.cur("SV"), "SS": sch.cur("SS"), "ST": sch.cur("ST")}
        for e, own in (("vector", "SV"), ("scalar", "SS"), ("tensor", "ST")):
            for s, v in vals.items():
                if s != own:
                    sch.wait(e, s, v)

    def dequant_pair(pk, sc_f32, tmp8, jobs):
        def sl(t):
            return t[0:pk.shape[0], 0:pk.shape[1]]
        regs = [[sl(tmp8[4 * j + i]) for i in range(4)] for j in range(len(jobs))]
        steps = []
        for j, (k, out_view) in enumerate(jobs):
            a, b, c, d = regs[j]
            shl = 22 - 4 * k
            if shl >= 0:
                steps.append((j, lambda e, a=a, shl=shl: e.tensor_scalar(
                    out=a, in0=pk, scalar1=shl, scalar2=0x03C00000,
                    op0=ALU.logical_shift_left, op1=ALU.bitwise_and)))
            else:
                steps.append((j, lambda e, a=a, shl=shl: e.tensor_scalar(
                    out=a, in0=pk, scalar1=-shl, scalar2=0x03C00000,
                    op0=ALU.logical_shift_right, op1=ALU.bitwise_and)))
            steps.append((j, lambda e, a=a, b=b: e.tensor_scalar(
                out=b, in0=a, scalar1=0x01C00000, scalar2=None, op0=ALU.bitwise_and)))
            steps.append((j, lambda e, a=a, c=c: e.tensor_scalar(
                out=c, in0=a, scalar1=0x02000000, scalar2=6,
                op0=ALU.bitwise_and, op1=ALU.logical_shift_left)))
            steps.append((j, lambda e, a=a, b=b: e.tensor_scalar(
                out=a, in0=b, scalar1=0x3F000000, scalar2=None, op0=ALU.add)))
            steps.append((j, lambda e, b=b, c=c: e.tensor_scalar(
                out=b, in0=c, scalar1=0x3F800000, scalar2=None, op0=ALU.add)))
            steps.append((j, lambda e, a=a, c=c: e.scalar_tensor_tensor(
                out=c.bitcast(F32), in0=a.bitcast(F32), scalar=1.0,
                in1=a.bitcast(F32), op0=ALU.min, op1=ALU.add)))
            steps.append((j, lambda e, b=b, c=c, d=d: e.scalar_tensor_tensor(
                out=d.bitcast(F32), in0=c.bitcast(F32), scalar=1.0,
                in1=b.bitcast(F32), op0=ALU.subtract, op1=ALU.mult)))
            dv = d.bitcast(F32)
            if len(out_view.shape) == 3:
                dv = dv.rearrange("p (b l) -> p b l", b=out_view.shape[1])
            steps.append((j, lambda e, ov=out_view, dv=dv: e.tensor_tensor(
                out=ov, in0=dv, in1=sc_f32, op=ALU.mult)))
        per = [[s for jj, s in steps if jj == j] for j in range(len(jobs))]
        n = max(len(p) for p in per)
        last = None
        for i in range(n):
            for j in range(len(jobs)):
                if i < len(per[j]):
                    last = per[j][i]
        for i in range(n):
            for j in range(len(jobs)):
                if i < len(per[j]):
                    fn = per[j][i]
                    if fn is last:
                        sch.do("vector", fn, "SV")
                    else:
                        sch.do("vector", fn)

    sch.do("vector", lambda e: e.memset(ones_red[:, :], 1.0))
    sch.do("vector", lambda e: e.memset(ones_bc[0:1, :], 1.0))
    sch.do("vector", lambda e: e.memset(A_acc[:, :], 0.0), "SV")
    for s in range(40, 48):
        sch.do("vector", lambda e, s=s: e.memset(h_bf[s][64:128, :], 0.0))
    c_bias = 0
    for dst, src in ((bqs, bq_d), (bks, bk_d), (bvs, bv_d), (bos, bo_d), (bms, bm_d)):
        c_bias = sq_dma(dst[:, :], src)
    for e in ("scalar", "vector", "tensor"):
        sch.wait(e, "SQ", c_bias)

    conv_jobs = ([("kv", oc) for oc in range(32)] + [("q", oc) for oc in range(CH)]
                 + [("out", oc) for oc in range(CH)])
    conv_ss = [0] * len(conv_jobs)
    conv_gq = [0] * len(conv_jobs)
    wkv_gq = [0] * 32
    wq_gq = [0] * CH
    wout_gq = [0] * CH

    def emit_convert(n):
        kind, oc = conv_jobs[n]
        src = {"kv": wkv_d, "q": wq_d, "out": wout_d}[kind]
        dst = {"kv": s_wkv, "q": s_wq, "out": s_wout}[kind]
        buf = n % 2
        if n >= 2:
            sch.wait("sync", "SS", conv_ss[n - 2])
        lv = sq_dma(cwin[buf][:, :], src[oc].rearrange("p c l -> p (c l)"))
        sch.wait("scalar", "SQ", lv)
        if n >= 2:
            sch.wait("scalar", "GQ", conv_gq[n - 2])
        conv_ss[n] = sch.do("scalar", lambda e, bb=buf: e.copy(
            out=cstage[bb][:, :], in_=cwin[bb][:, :]), "SS")
        sch.wait("gpsimd", "SS", conv_ss[n])
        conv_gq[n] = gq_dma(dst[oc], cstage[buf][:, :])
        {"kv": wkv_gq, "q": wq_gq, "out": wout_gq}[kind][oc] = conv_gq[n]

    for n in range(32):
        emit_convert(n)

    L_SPANS = [([2 * i, 2 * i + 1], 128) for i in range(5)] + [([10, 11], 64)]

    def emit_gu_local(si, ws):
        span, w = L_SPANS[si]
        nb = len(span)
        ncols = nb * w
        col0 = L_BLK[span[0]][2]
        st_cols = nb * CH * w
        stv = ws["stage"][:, 0:st_cols].rearrange("p (b c l) -> p b c l", b=nb, c=CH)
        if "last" in ws["sv"]:
            sch.wait("sync", "SV", ws["sv"]["last"])
        for J in range(2):
            sd_dma(ws["pk"][J][:, 0:ncols],
                   gup_d[J * 128:(J + 1) * 128, col0:col0 + ncols])
        l0 = sd_dma(ws["sc16"][:, 0:ncols], gsc_d[0:128, col0:col0 + ncols])
        sch.wait("vector", "SD", l0)
        sch.wait("vector", "GQ", ws["flush"][0])
        for J in range(2):
            cvv = sch.do("vector", lambda e, n_=ncols, ws=ws: e.tensor_copy(
                out=ws["sc"][:, 0:n_], in_=ws["sc16"][:, 0:n_]), "SV")
            if J == 0:
                sch.wait("sync", "SV", cvv)
                l1 = sd_dma(ws["sc16"][:, 0:ncols], gsc_d[128:256, col0:col0 + ncols])
            scv = ws["sc"][:, 0:ncols].rearrange("p (b l) -> p b l", b=nb)
            tmp8 = ws["tmp"] + gu_xtmp
            for k in range(0, 8, 2):
                dequant_pair(ws["pk"][J][:, 0:ncols], scv, tmp8,
                             [(k, stv[:, :, J * 8 + k, :]),
                              (k + 1, stv[:, :, J * 8 + k + 1, :])])
            if J == 0:
                sch.wait("vector", "SD", l1)
        ws["sv"]["last"] = sch.cur("SV")
        sch.wait("gpsimd", "SV", ws["sv"]["last"])
        out0 = L_BLK[span[0]][0]
        ws["flush"][0] = gq_dma(s_gu_my[:, out0:out0 + st_cols],
                                ws["stage"][:, 0:st_cols])

    def emit_dn_local(le, ws):
        e0 = le * 128
        stg = ws["stage"]
        sch.wait("vector", "GQ", ws["flush"][0])
        for s in range(40, 48):
            sch.do("vector", lambda e, s=s, stg=stg: e.memset(
                stg[64:128, s * 128:(s + 1) * 128], 0.0))
        for Jp in range(6):
            sz = 64 if Jp == 5 else 128
            par = Jp % 2
            sv = ws["sv"]
            if (le, Jp - 2) in sv:
                sch.wait("sync", "SV", sv[(le, Jp - 2)])
            elif (le - 1, 4 + Jp) in sv:
                sch.wait("sync", "SV", sv[(le - 1, 4 + Jp)])
            if ws["cv"][0]:
                sch.wait("sync", "SV", ws["cv"][0])
            sd_dma(ws["pk"][par][0:sz, 0:128],
                   dpk_d[Jp * 128:Jp * 128 + sz, e0:e0 + 128])
            l2 = sd_dma(ws["sc16"][0:sz, 0:128],
                        dsc_d[Jp * 128:Jp * 128 + sz, e0:e0 + 128])
            sch.wait("vector", "SD", l2)
            ws["cv"][0] = sch.do("vector", lambda e, sz=sz, ws=ws: e.tensor_copy(
                out=ws["sc"][0:sz, 0:128], in_=ws["sc16"][0:sz, 0:128]), "SV")
            tmp16 = ws["tmp"] + dn_xtmp
            for kp in range(0, 8, 4):
                s = Jp * 8 + kp
                dequant_pair(ws["pk"][par][0:sz, 0:128], ws["sc"][0:sz, 0:128], tmp16,
                             [(kp + j, stg[0:sz, (s + j) * 128:(s + j + 1) * 128])
                              for j in range(4)])
            sv[(le, Jp)] = sch.cur("SV")
        sch.wait("gpsimd", "SV", ws["sv"][(le, 5)])
        ws["flush"][0] = gq_dma(s_dn_my[e0:e0 + 128, :], stg[:, 0:N_SLOT * 128])
        if debug_outs:
            ws["flush"][0] = gq_dma(dbg[f"dnstg{le}"], stg[:, 0:N_SLOT * 128])

    def emit_ln(src_tiles, tcols, out_tiles):
        nh = tcols // T1
        for h in range(nh):
            for c in range(CH):
                sch.do("tensor", lambda e, c=c, h=h, t=src_tiles[c]: e.matmul(
                    psum[0 + h][0:1, 0:T1], ones_red[:, 0:1], t[:, h * T1:(h + 1) * T1],
                    start=(c == 0), stop=(c == CH - 1)),
                    "ST" if c == CH - 1 else None)
        sq_mm = {}
        prev = None
        for h in range(nh):
            for c in range(CH):
                bb = sqb[c % 2]
                key = (h, c - 2) if c >= 2 else ((h - 1, CH - 2 + c) if h > 0 else None)
                if key in sq_mm:
                    sch.wait("scalar", "ST", sq_mm[key])
                vs = sch.do("scalar", lambda e, t=src_tiles[c], bb=bb, h=h: e.activation(
                    out=bb[:, 0:T1], in_=t[:, h * T1:(h + 1) * T1], func=AF.Square), "SS")
                sch.wait("tensor", "SS", vs)
                sq_mm[(h, c)] = sch.do("tensor", lambda e, c=c, h=h, bb=bb: e.matmul(
                    psum[2 + h][0:1, 0:T1], ones_red[:, 0:1], bb[:, 0:T1],
                    start=(c == 0), stop=(c == CH - 1)), "ST")
        sch.wait("vector", "ST", sch.cur("ST"))
        m_ = ln_s[0][0:1, 0:tcols]
        v_ = ln_s[1][0:1, 0:tcols]
        for h in range(nh):
            sch.do("vector", lambda e, h=h: e.tensor_scalar(
                out=m_[0:1, h * T1:(h + 1) * T1], in0=psum[0 + h][0:1, 0:T1],
                scalar1=1.0 / D, scalar2=None, op0=ALU.mult))
            sch.do("vector", lambda e, h=h: e.tensor_scalar(
                out=v_[0:1, h * T1:(h + 1) * T1], in0=psum[2 + h][0:1, 0:T1],
                scalar1=1.0 / D, scalar2=None, op0=ALU.mult))
        mv = sch.do("vector", lambda e: e.tensor_copy(out=mb_bf[0:1, 0:tcols], in_=m_), "SV")
        m2 = rs_bf[0:1, 0:tcols]
        sch.do("vector", lambda e: e.scalar_tensor_tensor(
            out=m2, in0=m_, scalar=1.0, in1=m_, op0=ALU.mult, op1=ALU.mult))
        vv = sch.do("vector", lambda e: e.scalar_tensor_tensor(
            out=v_, in0=v_, scalar=EPS_LN, in1=m2, op0=ALU.add, op1=ALU.subtract), "SV")
        sch.wait("scalar", "SV", vv)
        sch.do("scalar", lambda e: e.activation(out=v_, in_=v_, func=AF.Ln))
        s2 = sch.do("scalar", lambda e: e.activation(
            out=rs_bf[0:1, 0:tcols], in_=v_, func=AF.Exp, scale=-0.5), "SS")
        sch.wait("tensor", "SV", mv)
        sch.wait("tensor", "SS", s2)
        bcv = None
        for h in range(nh):
            sch.do("tensor", lambda e, h=h: e.matmul(
                psum[4 + h][:, 0:T1], ones_bc[0:1, :], mb_bf[0:1, h * T1:(h + 1) * T1],
                start=True, stop=True))
            bcv = sch.do("tensor", lambda e, h=h: e.matmul(
                psum[6 + h][:, 0:T1], ones_bc[0:1, :], rs_bf[0:1, h * T1:(h + 1) * T1],
                start=True, stop=True), "ST")
        sch.wait("vector", "ST", bcv)
        for h in range(nh):
            for c in range(CH):
                o = out_tiles[c][:, h * T1:(h + 1) * T1] if tcols > T1 else out_tiles[c]
                sch.do("vector", lambda e, o=o, h=h, t=src_tiles[c]: e.tensor_tensor(
                    out=o, in0=t[:, h * T1:(h + 1) * T1], in1=psum[4 + h][:, 0:T1],
                    op=ALU.subtract))
                sch.do("vector", lambda e, o=o, h=h: e.tensor_tensor(
                    out=o, in0=o, in1=psum[6 + h][:, 0:T1], op=ALU.mult),
                    "SV" if (h == nh - 1 and c == CH - 1) else None)
        return sch.cur("SV")

    slab_ring = {"n": 0, "sq": [0, 0], "st": [0, 0]}

    def load_slab(src_ap, ncols):
        n = slab_ring["n"]
        buf = n % 2
        if n >= 2:
            sch.wait("sync", "ST", slab_ring["st"][buf])
        slab_ring["sq"][buf] = sq_dma(slab[buf][:, 0:ncols], src_ap)
        slab_ring["n"] = n + 1
        return buf

    gu_emitted = 0
    dn_emitted = 0
    sc_gather = [0, 0]
    ev_par = [0, 0]
    for t in range(n_p1):
        t0 = t * T1
        sch.wait("gpsimd", "SV", sch.cur("SV"))
        for c in range(CH):
            gq_dma(xbf[c][:, :], xT[c * 128:(c + 1) * 128, t0:t0 + T1])
        for i in range(4):
            gq_dma(cosb[i][:, :], cos_d[i, :, t0:t0 + T1])
            gq_dma(sinb[i][:, :], sin_d[i, :, t0:t0 + T1])
        gqv = sch.cur("GQ")
        for e in ("scalar", "tensor", "vector"):
            sch.wait(e, "GQ", gqv)

        lnv = emit_ln(xbf, T1, ln1)
        sch.wait("tensor", "SV", lnv)
        if debug_outs and t == 0:
            sch.wait("gpsimd", "SV", lnv)
            for c in range(CH):
                gq_dma(dbg["ln1"][c * 128:(c + 1) * 128, :], ln1[c])

        for oc in range(32):
            sch.wait("sync", "GQ", wkv_gq[oc])
            buf = load_slab(s_wkv[oc], CH * 128)
            sch.wait("tensor", "SQ", slab_ring["sq"][buf])
            par = oc % 2
            pb = psum[4 + par]
            sch.wait("tensor", "SS", ev_par[par])
            st_val = None
            for c in range(CH):
                st_val = sch.do("tensor", lambda e, bb=buf, c=c, pb=pb: e.matmul(
                    pb[:, 0:T1], slab[bb][:, c * 128:(c + 1) * 128], ln1[c],
                    start=(c == 0), stop=(c == CH - 1)),
                    "ST" if c == CH - 1 else None)
            slab_ring["st"][buf] = st_val
            sch.wait("scalar", "ST", st_val)
            if oc < CH:
                dst = kf[oc] if oc < 8 else kbf[oc - 8]
                bias, occ = bks, oc
            else:
                dst, bias, occ = vbf[oc - CH], bvs, oc - CH
            ev_par[par] = sch.do("scalar", lambda e, pb=pb, d_=dst, bi=bias, o=occ: e.activation(
                out=d_[:, 0:T1], in_=pb[:, 0:T1], func=AF.Identity,
                bias=bi[:, o:o + 1]), "SS")
        if debug_outs and t == 0:
            sch.wait("gpsimd", "SS", sch.cur("SS"))
            gq_dma(dbg["kf0"], kf[0][:, :])
            gq_dma(dbg["v0"], vbf[0][:, :])

        sch.wait("vector", "SS", sch.cur("SS"))
        for k in range(8):
            cc = k // 2
            op2 = ALU.subtract if k % 2 == 0 else ALU.add
            sch.do("vector", lambda e, k=k, cc=cc: e.tensor_tensor(
                out=krot_p1[k][:, :], in0=kf[k ^ 1][:, :], in1=sinb[cc][:, :], op=ALU.mult))
            sch.do("vector", lambda e, k=k, cc=cc: e.tensor_tensor(
                out=junk[:, :], in0=kf[k][:, :], in1=cosb[cc][:, :], op=ALU.mult))
            sch.do("vector", lambda e, k=k, op2=op2: e.tensor_tensor(
                out=krot_p1[k][:, :], in0=junk[:, :], in1=krot_p1[k][:, :], op=op2),
                "SV" if k == 7 else None)
        rope_v = sch.cur("SV")

        sch.wait("scalar", "SV", rope_v)
        kq_mm = {}
        for c in range(CH):
            src = krot_p1[c] if c < 8 else kbf[c - 8]
            bb = sqb[c % 2]
            if c >= 2:
                sch.wait("scalar", "ST", kq_mm[c - 2])
            vs = sch.do("scalar", lambda e, s_=src, bb=bb: e.activation(
                out=bb[:, 0:T1], in_=s_[:, 0:T1], func=AF.Square), "SS")
            sch.wait("tensor", "SS", vs)
            kq_mm[c] = sch.do("tensor", lambda e, c=c, bb=bb: e.matmul(
                psum[0][0:1, 0:T1], ones_red[:, 0:1], bb[:, 0:T1],
                start=(c == 0), stop=(c == CH - 1)), "ST")
        sch.wait("vector", "ST", sch.cur("ST"))
        ssq = ln_s[0][0:1, 0:T1]
        sv1 = sch.do("vector", lambda e: e.tensor_scalar(
            out=ssq, in0=psum[0][0:1, 0:T1], scalar1=1e-24, scalar2=None, op0=ALU.max), "SV")
        sch.wait("scalar", "SV", sv1)
        sch.do("scalar", lambda e: e.activation(out=ssq, in_=ssq, func=AF.Ln))
        rv = sch.do("scalar", lambda e: e.activation(
            out=rn_bf[0:1, 0:T1], in_=ssq, func=AF.Exp, scale=-0.5), "SS")
        sch.wait("tensor", "SS", rv)
        bc = sch.do("tensor", lambda e: e.matmul(
            psum[1][:, 0:T1], ones_bc[0:1, :], rn_bf[0:1, 0:T1], start=True, stop=True), "ST")
        sch.wait("vector", "ST", bc)

        for c in range(CH):
            ksrc = krot_p1[c] if c < 8 else kbf[c - 8]
            sch.do("vector", lambda e, c=c: e.tensor_tensor(
                out=vt[:, :], in0=vbf[c][:, :], in1=psum[1][:, 0:T1], op=ALU.mult))
            sch.do("vector", lambda e, c=c, ks=ksrc: e.scalar_tensor_tensor(
                out=junk[:, :], in0=ks[:, 0:T1], scalar=1.0, in1=vt[:, :],
                op0=ALU.mult, op1=ALU.mult, accum_out=A_cols[:, c:c + 1]))
        aav = sch.do("vector", lambda e: e.tensor_tensor(
            out=A_acc[:, :], in0=A_acc[:, :], in1=A_cols[:, :], op=ALU.add), "SV")
        sch.wait("tensor", "SV", aav)

        if t >= 1:
            cstart = 32 + (t - 1) * 5
            for n in range(cstart, min(cstart + 5, len(conv_jobs))):
                emit_convert(n)
        if dn_emitted < 2:
            emit_dn_local(dn_emitted, dn_ws_a)
            dn_emitted += 1
        elif gu_emitted < len(L_SPANS):
            emit_gu_local(gu_emitted, gu_ws_x)
            gu_emitted += 1
        if t == n_p1 - 1 and gu_emitted == len(L_SPANS) and dn_emitted == 2:
            sch.wait("gpsimd", "GQ", sch.cur("GQ"))
            sc_gather[0] = sch.do("gpsimd", lambda e: e.collective_compute(
                "AllGather", ALU.bypass, replica_groups=[list(range(NCORES))],
                ins=[s_gu_my.opt()], outs=[s_gu_full.opt()]), "SC")
            sch.wait("gpsimd", "SC", sc_gather[0])
            sc_gather[1] = sch.do("gpsimd", lambda e: e.collective_compute(
                "AllGather", ALU.bypass, replica_groups=[list(range(NCORES))],
                ins=[s_dn_my.opt()], outs=[s_dn_full.opt()]), "SC")

        barrier()

    for n in range(32 + (n_p1 - 1) * 5, len(conv_jobs)):
        emit_convert(n)

    sch.do("vector", lambda e: e.tensor_scalar(
        out=a01[:, 0:CH], in0=A_acc[:, :], scalar1=bms[:, 0:1], scalar2=None, op0=ALU.mult))
    av = sch.do("vector", lambda e: e.tensor_scalar(
        out=a01[:, CH:2 * CH], in0=A_acc[:, :], scalar1=bms[:, 1:2], scalar2=None,
        op0=ALU.mult), "SV")
    sch.wait("gpsimd", "SV", av)
    if debug_outs:
        gq_dma(dbg["a01w"], a01[:, :])
    ccsv = cc_src.rearrange("r (p c) -> r p c", p=128)
    gq_dma(ccsv[0], a01[:, 0:CH])
    g1 = gq_dma(ccsv[1], a01[:, CH:2 * CH])
    sch.wait("gpsimd", "GQ", g1)
    sch.wait("gpsimd", "SC", sc_gather[1])
    ccv = sch.do("gpsimd", lambda e: e.collective_compute(
        "AllReduce", ALU.add, replica_groups=[list(range(NCORES))],
        ins=[cc_src.opt()], outs=[cc_dst.opt()]), "SC")
    sch.wait("gpsimd", "SC", ccv)
    g2 = gq_dma(a01[:, 0:2 * CH].rearrange("p (r c) -> p r c", r=2),
                cc_dst.rearrange("r (p c) -> p r c", p=128))
    sch.wait("vector", "GQ", g2)
    jm = junk[:, 0:CH]
    jt = junk[:, CH:2 * CH]
    sch.do("vector", lambda e: e.tensor_scalar(
        out=jm, in0=a01[:, CH:2 * CH], scalar1=bms[:, 1:2], scalar2=None,
        op0=ALU.mult))
    sch.do("vector", lambda e: e.tensor_scalar(
        out=jt, in0=a01[:, 0:CH], scalar1=bms[:, 0:1], scalar2=None, op0=ALU.mult))
    sch.do("vector", lambda e: e.tensor_copy(
        out=junk[:, 128:256], in_=junk[:, 256:384]))
    sch.do("vector", lambda e: e.tensor_tensor(
        out=A_fin[:, :], in0=jt, in1=jm, op=ALU.add))
    sch.do("vector", lambda e: e.tensor_copy(
        out=junk[:, 128:256], in_=junk[:, 256:384]))
    afv = sch.do("vector", lambda e: e.tensor_tensor(
        out=ab_q[:, :], in0=A_fin[:, :], in1=bqs[:, :], op=ALU.mult), "SV")
    if debug_outs:
        sch.wait("gpsimd", "SV", afv)
        gq_dma(dbg["A_pre"], A_acc[:, :])
        gq_dma(dbg["A_fin"], A_fin[:, :])
        gq_dma(dbg["ccdst"], cc_dst)
        gq_dma(dbg["a01r"], a01[:, :])

    sch.wait("vector", "GQ", gu_ws_x["flush"][0])
    sch.wait("vector", "SQ", sch.cur("SQ"))
    sch.wait("vector", "SS", sch.cur("SS"))

    vt_guard = [0, 0]
    dn_ring = {"n": 0, "st": [0, 0]}
    out_sq = [0, 0]
    hv_par = [0, 0]
    yev_par = [0, 0]

    for t in range(n_p2):
        for sub in range(2):
            ts0 = t * T2 + sub * T1
            sch.wait("gpsimd", "SV", sch.cur("SV"))
            for c in range(CH):
                gq_dma(xbf[c][:, :], xT[c * 128:(c + 1) * 128, ts0:ts0 + T1])
            for i in range(4):
                gq_dma(cosb[i][:, :], cos_d[i, :, ts0:ts0 + T1])
                gq_dma(sinb[i][:, :], sin_d[i, :, ts0:ts0 + T1])
            gqv = sch.cur("GQ")
            for e in ("scalar", "tensor", "vector"):
                sch.wait(e, "GQ", gqv)

            lnv = emit_ln(xbf, T1, ln1)
            sch.wait("tensor", "SV", lnv)

            for oc in range(CH):
                sch.wait("sync", "GQ", wq_gq[oc])
                buf = load_slab(s_wq[oc], CH * 128)
                sch.wait("tensor", "SQ", slab_ring["sq"][buf])
                par = oc % 2
                pb = psum[4 + par]
                sch.wait("tensor", "SS", ev_par[par])
                st_val = None
                for c in range(CH):
                    st_val = sch.do("tensor", lambda e, bb=buf, c=c, pb=pb: e.matmul(
                        pb[:, 0:T1], slab[bb][:, c * 128:(c + 1) * 128], ln1[c],
                        start=(c == 0), stop=(c == CH - 1)),
                        "ST" if c == CH - 1 else None)
                slab_ring["st"][buf] = st_val
                sch.wait("scalar", "ST", st_val)
                if oc < 8:
                    ev_par[par] = sch.do("scalar", lambda e, pb=pb, oc=oc: e.activation(
                        out=qf[oc][:, 0:T1], in_=pb[:, 0:T1], func=AF.Identity,
                        bias=bqs[:, oc:oc + 1]), "SS")
                else:
                    if t == 0 and sub == 0 and oc == 8:
                        sch.wait("scalar", "SV", afv)
                    ev_par[par] = sch.do("scalar", lambda e, pb=pb, oc=oc: e.activation(
                        out=yd[oc][:, 0:T1], in_=pb[:, 0:T1], func=AF.Identity,
                        scale=A_fin[:, oc:oc + 1], bias=ab_q[:, oc:oc + 1]), "SS")

            sch.wait("vector", "SS", sch.cur("SS"))
            for k in range(8):
                cc = k // 2
                op2 = ALU.subtract if k % 2 == 0 else ALU.add
                sch.do("vector", lambda e, k=k, cc=cc: e.tensor_tensor(
                    out=krot_p2[k][:, :], in0=qf[k ^ 1][:, :], in1=sinb[cc][:, :], op=ALU.mult))
                sch.do("vector", lambda e, k=k, cc=cc: e.tensor_tensor(
                    out=junk[:, :], in0=qf[k][:, :], in1=cosb[cc][:, :], op=ALU.mult))
                sch.do("vector", lambda e, k=k, op2=op2: e.tensor_tensor(
                    out=krot_p2[k][:, :], in0=junk[:, :], in1=krot_p2[k][:, :], op=op2))
                sch.do("vector", lambda e, k=k: e.tensor_scalar(
                    out=yd[k][:, 0:T1], in0=krot_p2[k][:, :], scalar1=A_fin[:, k:k + 1],
                    scalar2=None, op0=ALU.mult), "SV" if k == 7 else None)
            ydv = sch.cur("SV")
            sch.wait("tensor", "SV", ydv)

            vv_par = [0, 0]
            for oc in range(CH):
                sch.wait("sync", "GQ", wout_gq[oc])
                buf = load_slab(s_wout[oc], CH * 128)
                sch.wait("tensor", "SQ", slab_ring["sq"][buf])
                par = oc % 2
                pb = psum[4 + par]
                sch.wait("tensor", "SV", vv_par[par])
                st_val = None
                for c in range(CH):
                    st_val = sch.do("tensor", lambda e, bb=buf, c=c, pb=pb: e.matmul(
                        pb[:, 0:T1], slab[bb][:, c * 128:(c + 1) * 128], yd[c][:, 0:T1],
                        start=(c == 0), stop=(c == CH - 1)),
                        "ST" if c == CH - 1 else None)
                slab_ring["st"][buf] = st_val
                sch.wait("vector", "ST", st_val)
                x1v = x1_bf[oc][:, sub * T1:(sub + 1) * T1]
                sch.do("vector", lambda e, pb=pb, oc=oc: e.tensor_tensor(
                    out=junk[:, :], in0=pb[:, 0:T1], in1=xbf[oc][:, :], op=ALU.add))
                vv_par[par] = sch.do("vector", lambda e, oc=oc, x1v=x1v: e.scalar_tensor_tensor(
                    out=x1v, in0=junk[:, :], scalar=bos[:, oc:oc + 1],
                    in1=ln1[oc], op0=ALU.add, op1=ALU.add), "SV")

            barrier()

        if t == 0:
            sch.wait("vector", "GQ", dn_ws_a["flush"][0])
        lnv2 = emit_ln(x1_bf, T2, ln_bf)
        sch.wait("tensor", "SV", lnv2)
        if debug_outs and t == 0:
            sch.wait("gpsimd", "SV", lnv2)
            for c in range(CH):
                gq_dma(dbg["x1"][c * 128:(c + 1) * 128, :], x1_bf[c][:, :])
                gq_dma(dbg["ln2"][c * 128:(c + 1) * 128, :], ln_bf[c][:, :])

        for s, (Jp, kp, sz) in enumerate(I_SLOTS):
            if t == 0 and s == 0:
                sch.wait("sync", "SC", sc_gather[0])
            gr, gc = gu_block_addr(2 * s)
            ur, uc = gu_block_addr(2 * s + 1)
            bufg = load_slab(s_gu_full[gr:gr + 128, gc:gc + CH * sz], CH * sz)
            bufu = load_slab(s_gu_full[ur:ur + 128, uc:uc + CH * sz], CH * sz)
            sch.wait("tensor", "SQ", slab_ring["sq"][bufu])
            q4 = (s % 2) * 4
            pg = [psum[q4 + 0], psum[q4 + 1]]
            pu = [psum[q4 + 2], psum[q4 + 3]]
            sch.wait("tensor", "SV", hv_par[s % 2])
            st_g = None
            for half in range(2):
                for c in range(CH):
                    st_g = sch.do("tensor", lambda e, bb=bufg, c=c, h=half, pg=pg, sz=sz: e.matmul(
                        pg[h][0:sz, :], slab[bb][:, c * sz:(c + 1) * sz],
                        ln_bf[c][:, h * T1:(h + 1) * T1],
                        start=(c == 0), stop=(c == CH - 1)),
                        "ST" if (half == 1 and c == CH - 1) else None)
            slab_ring["st"][bufg] = st_g
            st_u = None
            for half in range(2):
                for c in range(CH):
                    st_u = sch.do("tensor", lambda e, bb=bufu, c=c, h=half, pu=pu, sz=sz: e.matmul(
                        pu[h][0:sz, :], slab[bb][:, c * sz:(c + 1) * sz],
                        ln_bf[c][:, h * T1:(h + 1) * T1],
                        start=(c == 0), stop=(c == CH - 1)),
                        "ST" if (half == 1 and c == CH - 1) else None)
            slab_ring["st"][bufu] = st_u
            sch.wait("scalar", "ST", st_u)
            for half in range(2):
                vpar = (2 * s + half) % 2
                vbuf = vt if vpar == 0 else vt2
                sch.wait("scalar", "SV", vt_guard[vpar])
                sv = sch.do("scalar", lambda e, h=half, pg=pg, sz=sz, vb=vbuf: e.activation(
                    out=vb[0:sz, 0:T1], in_=pg[h][0:sz, :], func=AF.Silu), "SS")
                sch.wait("vector", "SS", sv)
                hv_par[s % 2] = sch.do("vector", lambda e, h=half, s=s, pu=pu, sz=sz, vb=vbuf: e.tensor_tensor(
                    out=h_bf[s][0:sz, h * T1:(h + 1) * T1], in0=vb[0:sz, 0:T1],
                    in1=pu[h][0:sz, :], op=ALU.mult), "SV")
                vt_guard[vpar] = hv_par[s % 2]
        if debug_outs and t == 0:
            sch.wait("gpsimd", "SV", sch.cur("SV"))
            gq_dma(dbg["h0"], h_bf[0][:, :])
            for s in range(N_SLOT):
                gq_dma(dbg["h_all"][s * 128:(s + 1) * 128, :], h_bf[s][:, :])
            gq_dma(dbg["wdn0"], s_dn_full[0:128, :])
            gq_dma(dbg["wgu0"], s_gu_full[0:128, 0:CH * 128])
            gq_dma(dbg["wdnall"], s_dn_full)
            gq_dma(dbg["wdnmy"], s_dn_my)
            gq_dma(dbg["wguall"], s_gu_full)

        sch.wait("tensor", "SV", sch.cur("SV"))
        if t == 0:
            sch.wait("scalar", "GQ", dn_ws_a["flush"][0])
        HS = N_SLOT // 2
        for ec in range(CH):
            if t == 0 and ec == 0:
                sch.wait("sync", "SC", sc_gather[1])
            par = ec % 2
            py = [psum[4 + 2 * par], psum[5 + 2 * par]]
            sch.wait("tensor", "SS", yev_par[par])
            st_val = None
            for hs in range(2):
                n = dn_ring["n"]
                buf = n % 2
                if n >= 2:
                    sch.wait("sync", "ST", dn_ring["st"][buf])
                dv = sq_dma(dn_slab[buf][:, :],
                            s_dn_full[ec * 128:(ec + 1) * 128,
                                      hs * HS * 128:(hs + 1) * HS * 128])
                dn_ring["n"] = n + 1
                sch.wait("tensor", "SQ", dv)
                for half in range(2):
                    for si in range(HS):
                        s = hs * HS + si
                        st_val = sch.do("tensor", lambda e, bb=buf, s=s, si=si, h=half, py=py: e.matmul(
                            py[h][:, :], dn_slab[bb][:, si * 128:(si + 1) * 128],
                            h_bf[s][:, h * T1:(h + 1) * T1],
                            start=(s == 0), stop=(s == N_SLOT - 1)),
                            "ST" if (h_bf and half == 1 and si == HS - 1) else None)
                dn_ring["st"][buf] = st_val
            sch.wait("scalar", "ST", st_val)
            for half in range(2):
                yev_par[par] = sch.do("scalar", lambda e, ec=ec, h=half, py=py: e.copy(
                    out=y_t[ec][:, h * T1:(h + 1) * T1], in_=py[h][:, :]), "SS")

        if debug_outs and t == 0:
            sch.wait("gpsimd", "SS", sch.cur("SS"))
            for c in range(CH):
                gq_dma(dbg["y0"][c * 128:(c + 1) * 128, :], y_t[c][:, :])
        sch.wait("scalar", "SS", sch.cur("SS"))
        ysq_mm = {}
        for h in range(2):
            for c in range(CH):
                bb = sqb[c % 2]
                key = (h, c - 2) if c >= 2 else ((h - 1, CH - 2 + c) if h > 0 else None)
                if key in ysq_mm:
                    sch.wait("scalar", "ST", ysq_mm[key])
                vs = sch.do("scalar", lambda e, c=c, h=h, bb=bb: e.activation(
                    out=bb[:, 0:T1], in_=y_t[c][:, h * T1:(h + 1) * T1], func=AF.Square), "SS")
                sch.wait("tensor", "SS", vs)
                ysq_mm[(h, c)] = sch.do("tensor", lambda e, c=c, h=h, bb=bb: e.matmul(
                    psum[0 + h][0:1, 0:T1], ones_red[:, 0:1], bb[:, 0:T1],
                    start=(c == 0), stop=(c == CH - 1)), "ST")
        sch.wait("vector", "ST", sch.cur("ST"))
        ri = ln_s[0][0:1, 0:T2]
        rv2 = None
        for h in range(2):
            rv2 = sch.do("vector", lambda e, h=h: e.tensor_scalar(
                out=ri[0:1, h * T1:(h + 1) * T1], in0=psum[0 + h][0:1, 0:T1],
                scalar1=1.0 / D, scalar2=EPS_RMS, op0=ALU.mult, op1=ALU.add),
                "SV" if h == 1 else None)
        sch.wait("scalar", "SV", rv2)
        sch.do("scalar", lambda e: e.activation(out=ri, in_=ri, func=AF.Ln))
        rr = sch.do("scalar", lambda e: e.activation(
            out=rn_bf[0:1, 0:T2], in_=ri, func=AF.Exp, scale=-0.5), "SS")
        sch.wait("tensor", "SS", rr)
        bcv = None
        for h in range(2):
            bcv = sch.do("tensor", lambda e, h=h: e.matmul(
                psum[2 + h][:, 0:T1], ones_bc[0:1, :], rn_bf[0:1, h * T1:(h + 1) * T1],
                start=True, stop=True), "ST")
        sch.wait("vector", "ST", bcv)

        for ec in range(CH):
            for h in range(2):
                buf = (2 * ec + h) % 2
                if ec >= 1:
                    sch.wait("vector", "SQ", out_sq[buf])
                sch.do("vector", lambda e, ec=ec, bb=buf, h=h: e.tensor_tensor(
                    out=ostage[bb][:, :],
                    in0=y_t[ec][:, h * T1:(h + 1) * T1],
                    in1=psum[2 + h][:, 0:T1], op=ALU.mult))
                ov = sch.do("vector", lambda e, ec=ec, bb=buf, h=h: e.tensor_tensor(
                    out=ostage[bb][:, :], in0=ostage[bb][:, :],
                    in1=x1_bf[ec][:, h * T1:(h + 1) * T1], op=ALU.add), "SV")
                sch.wait("sync", "SV", ov)
                out_sq[buf] = sq_dma(
                    outT[ec * 128:(ec + 1) * 128, t * T2 + h * T1:t * T2 + (h + 1) * T1],
                    ostage[buf][:, :])

        barrier()

    sch.wait("sync", "SQ", sch.cur("SQ"))
    sch.wait("sync", "GQ", sch.cur("GQ"))

    with nc.Block() as block:
        @block.sync
        def _(eng):
            for fn in sch.ops["sync"]:
                fn(eng)

        @block.vector
        def _(eng):
            for fn in sch.ops["vector"]:
                fn(eng)

        @block.scalar
        def _(eng):
            for fn in sch.ops["scalar"]:
                fn(eng)

        @block.tensor
        def _(eng):
            for fn in sch.ops["tensor"]:
                fn(eng)

        @block.gpsimd
        def _(eng):
            for fn in sch.ops["gpsimd"]:
                fn(eng)

    ctx.close()
    return nc


def kernel(L0_attn_X, L0_attn_W_in, L0_attn_b_in, L0_attn_W_out, L0_attn_b_out,
           L0_ffn_gate_up_packed, L0_ffn_gate_up_scales,
           L0_ffn_down_packed, L0_ffn_down_scales):
    in_maps = host_prep(L0_attn_X, L0_attn_W_in, L0_attn_b_in, L0_attn_W_out,
                        L0_attn_b_out, L0_ffn_gate_up_packed, L0_ffn_gate_up_scales,
                        L0_ffn_down_packed, L0_ffn_down_scales)
    nc = build_nc()
    res = run_bass_kernel_spmd(nc, in_maps, core_ids=list(range(NCORES)))
    out = np.empty((B, S, D), np.float32)
    P = P2048
    for core in range(NCORES):
        bb = core // 4
        s0 = (core % 4) * T_CORE
        ot = np.asarray(res.results[core]["outT"])
        out[bb, s0:s0 + T_CORE][:, P] = ot.T
    return out


if __name__ == "__main__":
    nc = build_nc()
    print("built ok, instructions:",
          sum(1 for _ in nc.all_instructions()) if hasattr(nc, "all_instructions") else "?")
